# version 01
# baseline (speedup 1.0000x reference)
import numpy as np
import ml_dtypes
import concourse.bass as bass
import concourse.mybir as mybir
from concourse.bass_utils import run_bass_kernel_spmd
from concourse.aot_env import allow_ici_all_to_all
from contextlib import ExitStack

F32 = mybir.dt.float32
BF16 = mybir.dt.bfloat16
I32 = mybir.dt.int32
AF = mybir.ActivationFunctionType
ALU = mybir.AluOpType
AX = mybir.AxisListType.X

NCORES = 8
B, S, D = 16, 2048, 1024
BPC = 2
SPC = S // NCORES
TOK0 = BPC * S
TOK1 = B * SPC
NH = 16
DK = 64
FH = 3072
E, TOPK, NG = 64, 8, 8
CAP = 1024
PAD = 192
EBIG = 20000.0
NEG = -1e30


class Counts:
    def __init__(self):
        self.v = {}

    def inc(self, name, n):
        self.v[name] = self.v.get(name, 0) + n
        return self.v[name]

    def get(self, name):
        return self.v.get(name, 0)



def build_program(l0_only=False, l1_only=False):
    nc = bass.Bass("TRN2", target_bir_lowering=False, num_devices=NCORES)

    x_in = nc.dram_tensor("x_slice", [BPC, S, D], F32, kind="ExternalInput").ap()
    pq_in = nc.dram_tensor("L0_attn_P_q", [NH, D, DK], F32, kind="ExternalInput").ap()
    pk_in = nc.dram_tensor("L0_attn_P_k", [NH, D, DK], F32, kind="ExternalInput").ap()
    pv_in = nc.dram_tensor("L0_attn_P_v", [NH, D, DK], F32, kind="ExternalInput").ap()
    po_in = nc.dram_tensor("L0_attn_P_o", [NH, D, DK], F32, kind="ExternalInput").ap()
    wg_in = nc.dram_tensor("L0_ffn_w_gate", [FH, D], I32, kind="ExternalInput").ap()
    wu_in = nc.dram_tensor("L0_ffn_w_up", [FH, D], I32, kind="ExternalInput").ap()
    wd_in = nc.dram_tensor("L0_ffn_w_down", [D, FH], I32, kind="ExternalInput").ap()
    id_bf = nc.dram_tensor("id_bf", [128, 128], BF16, kind="ExternalInput").ap()
    id_f32 = nc.dram_tensor("id_f32", [128, 128], F32, kind="ExternalInput").ap()
    triu_bf = nc.dram_tensor("triu_bf", [128, 128], BF16, kind="ExternalInput").ap()
    tril_bf = nc.dram_tensor("tril_bf", [128, 128], BF16, kind="ExternalInput").ap()
    if not l0_only:
        wqkv_in = nc.dram_tensor("L1_attn_Wqkv", [3 * D, D], F32, kind="ExternalInput").ap()
        bqkv_in = nc.dram_tensor("L1_attn_bqkv", [1, 3 * D], F32, kind="ExternalInput").ap()
        gamma_in = nc.dram_tensor("L1_attn_seq_gamma", [1, NH], F32, kind="ExternalInput").ap()
        beta_in = nc.dram_tensor("L1_attn_seq_beta", [1, NH], F32, kind="ExternalInput").ap()
        gw_in = nc.dram_tensor("L1_ffn_gate_weight", [E, D], F32, kind="ExternalInput").ap()
        ebias_in = nc.dram_tensor("L1_ffn_e_bias", [1, E], F32, kind="ExternalInput").ap()
        w1_my = nc.dram_tensor("w1_my", [E // NCORES, 512, D], F32, kind="ExternalInput").ap()
        w2_my = nc.dram_tensor("w2_my", [E // NCORES, D, 512], F32, kind="ExternalInput").ap()
        w3_my = nc.dram_tensor("w3_my", [E // NCORES, 512, D], F32, kind="ExternalInput").ap()
        sg_in = nc.dram_tensor("L1_ffn_shared_gate", [D, D], F32, kind="ExternalInput").ap()
        su_in = nc.dram_tensor("L1_ffn_shared_up", [D, D], F32, kind="ExternalInput").ap()
        sd_in = nc.dram_tensor("L1_ffn_shared_down", [D, D], F32, kind="ExternalInput").ap()
        u_tile = nc.dram_tensor("u_tile", [128, 128], F32, kind="ExternalInput").ap()
        u_plain = nc.dram_tensor("u_plain", [128, 128], F32, kind="ExternalInput").ap()
        eb_mat = nc.dram_tensor("eb_mat", [128, 16], F32, kind="ExternalInput").ap()
        ebt_mat = nc.dram_tensor("ebt_mat", [16, 128], F32, kind="ExternalInput").ap()
        u1_mine = nc.dram_tensor("u1_mine", [128, 16], F32, kind="ExternalInput").ap()
        e192_row = nc.dram_tensor("e192_row", [1, E], F32, kind="ExternalInput").ap()
        bd16_mask = nc.dram_tensor("bd16_mask", [128, 512], BF16, kind="ExternalInput").ap()
        iota_col = nc.dram_tensor("iota_col", [128, 1], F32, kind="ExternalInput").ap()
        y_out = nc.dram_tensor("y_slice", [B, SPC, D], F32, kind="ExternalOutput").ap()
        wqkv_b = nc.dram_tensor("wqkv_b", [3 * D, D], BF16)
        sg_b = nc.dram_tensor("sg_b", [D, D], BF16)
        su_b = nc.dram_tensor("su_b", [D, D], BF16)
        sd_b = nc.dram_tensor("sd_b", [D, D], BF16)
        gw_bd = nc.dram_tensor("gw_bd", [E, D], BF16)
        moe1 = nc.dram_tensor("moe1", [E, 512, D], BF16)
        moe3 = nc.dram_tensor("moe3", [E, 512, D], BF16)
        moe2 = nc.dram_tensor("moe2", [E, D, 512], BF16)
        moe1_my = nc.dram_tensor("moe1_my", [E // NCORES, 512, D], BF16)
        moe3_my = nc.dram_tensor("moe3_my", [E // NCORES, 512, D], BF16)
        moe2_my = nc.dram_tensor("moe2_my", [E // NCORES, D, 512], BF16)
        xn1tab = nc.dram_tensor("xn1tab", [TOK1 + 1, D + E], BF16)
        yetab = nc.dram_tensor("yetab", [E * PAD + 1, D], BF16)
        part1 = nc.dram_tensor("part1", [TOK1, D], BF16)
        cnt_bounce = nc.dram_tensor("cnt_bounce", [16, E], F32)
        cnt_all = nc.dram_tensor("cnt_all", [NCORES * 16, E], F32)

    if l0_only:
        x1_out = nc.dram_tensor("x1_dbg", [TOK0, D], F32, kind="ExternalOutput").ap()
        attn_dbg = nc.dram_tensor("attn_dbg", [TOK0, D], BF16, kind="ExternalOutput").ap()
        ffnp_dbg = nc.dram_tensor("ffnp_dbg", [3, TOK0, D], BF16, kind="ExternalOutput").ap()
        xt0_dbg = nc.dram_tensor("xt0_dbg", [BPC, 8, 128, S], BF16, kind="ExternalOutput").ap()

    po_bf = nc.dram_tensor("po_bf", [D, NH * DK], BF16)
    wg_b = nc.dram_tensor("wg_b", [FH, D], BF16)
    wu_b = nc.dram_tensor("wu_b", [FH, D], BF16)
    wd_b = nc.dram_tensor("wd_b", [D, FH], BF16)
    xt0_dram = nc.dram_tensor("xt0_dram", [BPC, 8, 128, S], BF16)
    attn_rows = nc.dram_tensor("attn_rows", [TOK0, D], BF16)
    ffnp = nc.dram_tensor("ffnp", [3, TOK0, D], BF16)
    negm_stage = nc.dram_tensor("negm_stage", [16, 128], F32)
    a2a_in = nc.dram_tensor("a2a_in", [TOK0, D], F32)
    if not l0_only:
        if l1_only:
            a2a_out = nc.dram_tensor("x1_slice", [TOK1, D], F32, kind="ExternalInput").ap()
            dbg_xn1 = nc.dram_tensor("dbg_xn1", [TOK1 + 1, D + E], BF16, kind="ExternalOutput").ap()
            dbg_part1 = nc.dram_tensor("dbg_part1", [TOK1, D], BF16, kind="ExternalOutput").ap()
            dbg_ye = nc.dram_tensor("dbg_ye", [E * PAD + 1, D], BF16, kind="ExternalOutput").ap()
            dbg_af = nc.dram_tensor("dbg_af", [128, 2048], F32, kind="ExternalOutput").ap()
            dbg_dest = nc.dram_tensor("dbg_dest", [128, 256], I32, kind="ExternalOutput").ap()
            dbg_gidx = nc.dram_tensor("dbg_gidx", [128, 128], I32, kind="ExternalOutput").ap()
            dbg_cnt = nc.dram_tensor("dbg_cnt", [128, 192], F32, kind="ExternalOutput").ap()
        else:
            a2a_out_t = nc.dram_tensor("a2a_out", [TOK1, D], F32)
            a2a_out = a2a_out_t.ap()

    C = Counts()

    with ExitStack() as top:
        sem = {}
        for name in ["SY", "GP", "MM", "VE", "SE", "CC"]:
            sem[name] = top.enter_context(nc.semaphore("sem" + name))

        def inc(inst, name, n=1):
            inst.then_inc(sem[name], n)
            return C.inc(name, n)

        def hard_barrier0():
            for engname in ("tensor", "vector", "scalar", "sync", "gpsimd"):
                def _bar(eng):
                    for s_ in ("SY", "GP", "MM", "VE", "SE", "CC"):
                        eng.wait_ge(sem[s_], C.get(s_))
                getattr(block, engname)(_bar)

        PS = [top.enter_context(nc.psum_tensor(f"ps{i}", [128, 512], F32))
              for i in range(8)]

        def ps_bf(i):
            return PS[i][:, :].bitcast(BF16)

        block = top.enter_context(nc.Block())

        consts = top.enter_context(nc.sbuf_tensor("consts", [128, 128 * 3], BF16))
        idf_sb = top.enter_context(nc.sbuf_tensor("idf_sb", [128, 128], F32))
        ones_bf = top.enter_context(nc.sbuf_tensor("ones_bf", [128, 128], BF16))

        @block.sync
        def _(sy):
            inc(sy.dma_start(out=consts[:, 0:128], in_=id_bf), "SY", 16)
            inc(sy.dma_start(out=consts[:, 128:256], in_=triu_bf), "SY", 16)
            inc(sy.dma_start(out=consts[:, 256:384], in_=tril_bf), "SY", 16)
            inc(sy.dma_start(out=idf_sb[:, :], in_=id_f32), "SY", 16)

        epscol = top.enter_context(nc.sbuf_tensor("epscol", [128, 1], F32))

        @block.vector
        def _(v):
            inc(v.memset(ones_bf[:, :], 1.0), "VE")
            inc(v.memset(epscol[:, :], 1e-5), "VE")

        ID = consts[:, 0:128]
        TRIU = consts[:, 128:256]
        TRIL = consts[:, 256:384]

        if not l1_only:
            p01 = ExitStack()
            pqs = p01.enter_context(nc.sbuf_tensor("pqs", [128, 8192], BF16))
            pks = p01.enter_context(nc.sbuf_tensor("pks", [128, 8192], BF16))
            pvs = p01.enter_context(nc.sbuf_tensor("pvs", [128, 8192], BF16))
            pots = p01.enter_context(nc.sbuf_tensor("pots", [128, 8 * 1024], BF16))

            @block.gpsimd
            def _(g):
                for tgt, src in ((pqs, pq_in), (pks, pk_in), (pvs, pv_in)):
                    for h in range(NH):
                        inc(g.dma_start(
                            out=tgt[:, :].rearrange("p (a h v) -> p a h v", a=8, h=16)[:, :, h, :],
                            in_=src[h].rearrange("(a p) v -> p a v", p=128)), "GP", 16)
                for h in range(NH):
                    inc(g.dma_start(out=po_bf[:, h * DK:(h + 1) * DK]
                                    .rearrange("(a p) v -> p a v", p=128),
                                    in_=po_in[h].rearrange("(a p) v -> p a v", p=128)), "GP", 16)

            @block.vector
            def _(v):
                v.wait_ge(sem["GP"], C.get("GP"))
                inc(v.tensor_scalar_mul(pqs[:, :], pqs[:, :], 8.0), "VE")

            with ExitStack() as pp:
                istage = pp.enter_context(nc.sbuf_tensor("istage", [128, FH], F32))
                bstage = pp.enter_context(nc.sbuf_tensor("bstage", [128, FH], BF16))
                jobs = []
                for (dst, src, rows, cols) in ((wg_b, wg_in, FH, D), (wu_b, wu_in, FH, D),
                                               (wd_b, wd_in, D, FH)):
                    for r0 in range(0, rows, 128):
                        jobs.append((dst, src, r0, cols))
                for (dst, src, r0, cols) in jobs:
                    @block.gpsimd
                    def _(g):
                        g.wait_ge(sem["VE"], C.get("VE"))
                        inc(g.dma_start(out=istage[:, :cols], in_=src[r0:r0 + 128, :]), "GP", 16)

                    @block.vector
                    def _(v):
                        v.wait_ge(sem["GP"], C.get("GP"))
                        v.wait_ge(sem["SY"], C.get("SY"))
                        inc(v.tensor_copy(bstage[:, :cols], istage[:, :cols]), "VE")

                    @block.sync
                    def _(sy):
                        sy.wait_ge(sem["VE"], C.get("VE"))
                        inc(sy.dma_start(out=dst[r0:r0 + 128, :], in_=bstage[:, :cols]), "SY", 16)

                @block.gpsimd
                def _(g):
                    g.wait_ge(sem["SY"], C.get("SY"))

            @block.sync
            def _(sy):
                sy.wait_ge(sem["GP"], C.get("GP"))
                for hp in range(8):
                    inc(sy.dma_start(out=pots[:, hp * 1024:(hp + 1) * 1024],
                                     in_=po_bf[:, hp * 128:(hp + 1) * 128], transpose=True),
                        "SY", 16)

            xt0 = p01.enter_context(nc.sbuf_tensor("xt0", [128, 8 * S], BF16))
            vsb = p01.enter_context(nc.sbuf_tensor("vsb", [128, 16 * (NH * 65)], BF16))
            onorm = p01.enter_context(nc.sbuf_tensor("onorm", [128, 8 * S], BF16))
            qt8 = p01.enter_context(nc.sbuf_tensor("qt8", [128, S], BF16))
            kt8 = p01.enter_context(nc.sbuf_tensor("kt8", [128, S], BF16))
            expt = p01.enter_context(nc.sbuf_tensor("expt", [128, 2 * 512], BF16))
            negm = p01.enter_context(nc.sbuf_tensor("negm", [65, 2 * S], BF16))
            nmcols = p01.enter_context(nc.sbuf_tensor("nmcols", [128, 64], F32))
            lnrow = p01.enter_context(nc.sbuf_tensor("lnrow", [128, 1024], F32))
            lnsc = p01.enter_context(nc.sbuf_tensor("lnsc", [128, 8], F32))
            xnrow = p01.enter_context(nc.sbuf_tensor("xnrow", [128, 1024], BF16))
            otmp = p01.enter_context(nc.sbuf_tensor("otmp", [64, 512], BF16))
            rrow = p01.enter_context(nc.sbuf_tensor("rrow", [1, 512], BF16))

            for b in range(BPC):
                for tt in range(16):
                    @block.sync
                    def _(sy):
                        sy.wait_ge(sem["MM"], C.get("MM"))
                        inc(sy.dma_start(out=lnrow[:, :],
                                         in_=x_in[b, tt * 128:(tt + 1) * 128, :]), "SY", 16)

                    @block.scalar
                    def _(s):
                        s.wait_ge(sem["SY"], C.get("SY"))
                        s.activation(xnrow[:, :], lnrow[:, :], AF.Copy, scale=1.0 / D,
                                     accum_out=lnsc[:, 1:2])
                        inc(s.activation(xnrow[:, :], lnrow[:, :], AF.Square,
                                         scale=1.0 / float(np.sqrt(D)),
                                         accum_out=lnsc[:, 2:3]), "SE")

                    @block.vector
                    def _(v):
                        v.wait_ge(sem["SE"], C.get("SE"))
                        inc(v.tensor_tensor(lnsc[:, 3:4], lnsc[:, 1:2], lnsc[:, 1:2], op=ALU.mult), "VE")
                        v.wait_ge(sem["VE"], C.get("VE"))
                        inc(v.tensor_tensor(lnsc[:, 5:6], lnsc[:, 2:3], lnsc[:, 3:4],
                                            op=ALU.subtract), "VE")

                    @block.scalar
                    def _(s):
                        s.wait_ge(sem["VE"], C.get("VE"))
                        inc(s.activation(lnsc[:, 7:8], lnsc[:, 5:6], AF.Sqrt, bias=epscol[:, 0:1]), "SE")

                    @block.vector
                    def _(v):
                        v.wait_ge(sem["SE"], C.get("SE"))
                        inc(v.reciprocal(lnsc[:, 0:1], lnsc[:, 7:8]), "VE")
                        v.wait_ge(sem["VE"], C.get("VE"))
                        inc(v.tensor_scalar(xnrow[:, :], lnrow[:, :], lnsc[:, 1:2], lnsc[:, 0:1],
                                            op0=ALU.subtract, op1=ALU.mult), "VE")

                    @block.tensor
                    def _(t):
                        t.wait_ge(sem["VE"], C.get("VE"))
                        for dc in range(8):
                            last = t.transpose(ps_bf(7)[:, dc * 128:(dc + 1) * 128],
                                               xnrow[:, dc * 128:(dc + 1) * 128], ID)
                        inc(last, "MM")

                    @block.vector
                    def _(v):
                        v.wait_ge(sem["MM"], C.get("MM"))
                        inc(v.tensor_copy(
                            xt0[:, :].rearrange("p (a s) -> p a s", a=8)[:, :, tt * 128:(tt + 1) * 128],
                            ps_bf(7)[:, :].rearrange("p (a s) -> p a s", a=8)), "VE")

                @block.sync
                def _(sy):
                    sy.wait_ge(sem["VE"], C.get("VE"))
                    for dc in range(8):
                        inc(sy.dma_start(out=xt0_dram[b, dc], in_=xt0[:, dc * S:(dc + 1) * S]),
                            "SY", 16)

                for tt in range(16):
                    @block.tensor
                    def _(t):
                        for hg in range(2):
                            t.wait_ge(sem["VE"], C.get("VE"))
                            for dc in range(8):
                                last = t.matmul(PS[hg][:, :],
                                                xt0[:, dc * S + tt * 128: dc * S + (tt + 1) * 128],
                                                pvs[:, dc * 1024 + hg * 512: dc * 1024 + (hg + 1) * 512],
                                                start=(dc == 0), stop=(dc == 7))
                            inc(last, "MM")

                    @block.vector
                    def _(v):
                        base = tt * (NH * 65)
                        for hg in range(2):
                            v.wait_ge(sem["MM"], C.get("MM") - (1 - hg))
                            inc(v.tensor_copy(
                                vsb[:, base + hg * 8 * 65: base + (hg * 8 + 8) * 65]
                                .rearrange("p (h v) -> p h v", h=8)[:, :, 0:64],
                                PS[hg][:, :].rearrange("p (h v) -> p h v", h=8)), "VE")
                        inc(v.memset(vsb[:, base:base + NH * 65]
                                     .rearrange("p (h v) -> p h v", h=NH)[:, :, 64:65], 1.0), "VE")

                for hp in range(8):
                    for tc in range(4):
                        @block.tensor
                        def _(t):
                            t.wait_ge(sem["VE"], C.get("VE"))
                            for w_sb, bank in ((pqs, 2), (pks, 3)):
                                for dc in range(8):
                                    last = t.matmul(
                                        PS[bank][:, :],
                                        w_sb[:, dc * 1024 + hp * 128: dc * 1024 + hp * 128 + 128],
                                        xt0[:, dc * S + tc * 512: dc * S + (tc + 1) * 512],
                                        start=(dc == 0), stop=(dc == 7))
                                inc(last, "MM")

                        @block.vector
                        def _(v):
                            v.wait_ge(sem["MM"], C.get("MM"))
                            v.tensor_copy(qt8[:, tc * 512:(tc + 1) * 512], PS[2][:, :])
                            inc(v.tensor_copy(kt8[:, tc * 512:(tc + 1) * 512], PS[3][:, :]), "VE")

                    for qt in range(16):
                        nchunks = qt // 4 + 1

                        @block.tensor
                        def _(t):
                            t.wait_ge(sem["VE"], C.get("VE"))
                            for kc in range(nchunks):
                                n = 512 if kc < qt // 4 else (qt * 128 - kc * 512 + 128)
                                diag = (kc == qt // 4)
                                mmA = t.matmul(PS[kc][:, 0:n], qt8[0:64, qt * 128:(qt + 1) * 128],
                                               kt8[0:64, kc * 512: kc * 512 + n],
                                               start=True, stop=not diag, tile_position=(0, 0))
                                mmB = t.matmul(PS[4 + kc][:, 0:n], qt8[64:128, qt * 128:(qt + 1) * 128],
                                               kt8[64:128, kc * 512: kc * 512 + n],
                                               start=True, stop=not diag, tile_position=(64, 0))
                                if diag:
                                    off = qt * 128 - kc * 512
                                    mmA = t.matmul(PS[kc][:, off:off + 128], ID, TRIU,
                                                   start=False, stop=True)
                                    mmB = t.matmul(PS[4 + kc][:, off:off + 128], ID, TRIU,
                                                   start=False, stop=True)
                                inc(mmB, "MM")

                        @block.vector
                        def _(v):
                            for kc in range(nchunks):
                                n = 512 if kc < qt // 4 else (qt * 128 - kc * 512 + 128)
                                v.wait_ge(sem["MM"], C.get("MM") - (nchunks - 1 - kc))
                                v.reduce_max(lnsc[:, kc:kc + 1], PS[kc][:, 0:n], axis=AX)
                                inc(v.reduce_max(lnsc[:, 4 + kc:5 + kc], PS[4 + kc][:, 0:n],
                                                 axis=AX), "VE")
                            v.wait_ge(sem["VE"], C.get("VE"))
                            v.reduce_max(nmcols[:, qt:qt + 1], lnsc[:, 0:nchunks], axis=AX)
                            inc(v.reduce_max(nmcols[:, 16 + qt:17 + qt], lnsc[:, 4:4 + nchunks],
                                             axis=AX), "VE")

                    @block.vector
                    def _(v):
                        v.wait_ge(sem["VE"], C.get("VE"))
                        inc(v.tensor_scalar_mul(nmcols[:, 32:64], nmcols[:, 0:32], -1.0), "VE")
                        v.wait_ge(sem["VE"], C.get("VE"))

                    for hh in range(2):
                        @block.tensor
                        def _(t):
                            t.wait_ge(sem["VE"], C.get("VE"))
                            t.wait_ge(sem["GP"], C.get("GP"))
                            inc(t.transpose(PS[7][0:16, 0:128].bitcast(F32) if False else PS[7][0:16, 0:128],
                                            nmcols[:, 32 + hh * 16: 48 + hh * 16],
                                            idf_sb[:, :]), "MM")

                        @block.vector
                        def _(v):
                            v.wait_ge(sem["MM"], C.get("MM"))
                            inc(v.tensor_copy(lnrow[0:16, 0:128], PS[7][0:16, 0:128]), "VE")

                        @block.sync
                        def _(sy):
                            sy.wait_ge(sem["VE"], C.get("VE"))
                            inc(sy.dma_start(out=negm_stage.ap(), in_=lnrow[0:16, 0:128]), "SY", 16)

                        @block.gpsimd
                        def _(g):
                            g.wait_ge(sem["SY"], C.get("SY"))
                            inc(g.dma_start(
                                out=negm[hh * 64: hh * 64 + 1, hh * S: hh * S + S],
                                in_=negm_stage.ap().rearrange("a b -> (a b)")[None, :]), "GP", 16)

                    for qc in range(4):
                        nkt = 4 * qc + 4
                        for kt in range(nkt):
                            off = max(0, (kt - 4 * qc) * 128)
                            diag = (kt >= 4 * qc)

                            @block.tensor
                            def _(t):
                                t.wait_ge(sem["SE"], C.get("SE"))
                                if kt == 0:
                                    t.wait_ge(sem["VE"], C.get("VE"))
                                    t.wait_ge(sem["GP"], C.get("GP"))
                                for hh, pb in ((0, 2), (1, 3)):
                                    lo = hh * 64
                                    t.matmul(PS[pb][:, off:512],
                                             kt8[lo:lo + 64, kt * 128:(kt + 1) * 128],
                                             qt8[lo:lo + 64, qc * 512 + off: (qc + 1) * 512],
                                             start=True, stop=False, tile_position=(lo, 0))
                                    mm = t.matmul(PS[pb][:, off:512],
                                                  ones_bf[lo:lo + 1, 0:128],
                                                  negm[lo:lo + 1, hh * S + qc * 512 + off: hh * S + (qc + 1) * 512],
                                                  start=False, stop=not diag, tile_position=(lo, 0))
                                    if diag:
                                        mm = t.matmul(PS[pb][:, off:off + 128], ID, TRIL,
                                                      start=False, stop=True)
                                inc(mm, "MM")

                            @block.scalar
                            def _(s):
                                s.wait_ge(sem["MM"], C.get("MM"))
                                s.activation(expt[:, off:512], PS[2][:, off:512], AF.Exp)
                                inc(s.activation(expt[:, 512 + off:1024], PS[3][:, off:512], AF.Exp),
                                    "SE")

                            @block.tensor
                            def _(t):
                                t.wait_ge(sem["SE"], C.get("SE"))
                                first = (kt == 0)
                                lastk = (kt == nkt - 1)
                                vb = kt * (NH * 65)
                                t.matmul(PS[4][0:65, off:512],
                                         vsb[:, vb + (2 * hp) * 65: vb + (2 * hp) * 65 + 65],
                                         expt[:, off:512], start=first, stop=lastk)
                                inc(t.matmul(PS[5][0:65, off:512],
                                             vsb[:, vb + (2 * hp + 1) * 65: vb + (2 * hp + 1) * 65 + 65],
                                             expt[:, 512 + off:1024], start=first, stop=lastk), "MM")

                        for hh in range(2):
                            @block.vector
                            def _(v):
                                v.wait_ge(sem["MM"], C.get("MM"))
                                inc(v.reciprocal(lnrow[0:1, 0:512], PS[4 + hh][64:65, 0:512]), "VE")
                                v.wait_ge(sem["VE"], C.get("VE"))
                                inc(v.tensor_copy(rrow[0:1, 0:512], lnrow[0:1, 0:512]), "VE")

                            @block.tensor
                            def _(t):
                                t.wait_ge(sem["VE"], C.get("VE"))
                                inc(t.matmul(PS[6][0:64, 0:512], ones_bf[0:1, 0:64],
                                             rrow[0:1, 0:512], start=True, stop=True), "MM")

                            @block.scalar
                            def _(s):
                                s.wait_ge(sem["MM"], C.get("MM"))
                                inc(s.copy(otmp[0:64, 0:512], PS[4 + hh][0:64, 0:512]), "SE")

                            @block.vector
                            def _(v):
                                v.wait_ge(sem["SE"], C.get("SE"))
                                inc(v.tensor_tensor(
                                    onorm[hh * 64:(hh + 1) * 64, hp * S + qc * 512: hp * S + (qc + 1) * 512],
                                    otmp[0:64, 0:512], PS[6][0:64, 0:512], op=ALU.mult), "VE")

                for tt in range(16):
                    @block.tensor
                    def _(t):
                        t.wait_ge(sem["VE"], C.get("VE"))
                        t.wait_ge(sem["SY"], C.get("SY"))
                        for dh in range(2):
                            for hp in range(8):
                                last = t.matmul(PS[dh][:, :],
                                                onorm[:, hp * S + tt * 128: hp * S + (tt + 1) * 128],
                                                pots[:, hp * 1024 + dh * 512: hp * 1024 + (dh + 1) * 512],
                                                start=(hp == 0), stop=(hp == 7))
                            inc(last, "MM")

                    @block.vector
                    def _(v):
                        v.wait_ge(sem["MM"], C.get("MM"))
                        v.tensor_copy(xnrow[:, 0:512], PS[0][:, :])
                        inc(v.tensor_copy(xnrow[:, 512:1024], PS[1][:, :]), "VE")

                    @block.sync
                    def _(sy):
                        sy.wait_ge(sem["VE"], C.get("VE"))
                        inc(sy.dma_start(out=attn_rows[b * S + tt * 128: b * S + (tt + 1) * 128, :],
                                         in_=xnrow[:, :]), "SY", 16)

            hard_barrier0()
            p01.close()

            with ExitStack() as pf:
                wgt = pf.enter_context(nc.sbuf_tensor("wgt", [128, 8 * 1024], BF16))
                wut = pf.enter_context(nc.sbuf_tensor("wut", [128, 8 * 1024], BF16))
                wdt = pf.enter_context(nc.sbuf_tensor("wdt", [128, 8 * 1024], BF16))
                xtc = pf.enter_context(nc.sbuf_tensor("xtc", [128, 8 * 512], BF16))
                silt = pf.enter_context(nc.sbuf_tensor("silt", [128, 512], BF16))
                hglu = pf.enter_context(nc.sbuf_tensor("hglu", [128, 8 * 512], BF16))
                prow = pf.enter_context(nc.sbuf_tensor("prow", [128, 1024], BF16))

                for og in range(3):
                    @block.sync
                    def _(sy):
                        sy.wait_ge(sem["MM"], C.get("MM"))
                        for dc in range(8):
                            inc(sy.dma_start(out=wgt[:, dc * 1024:(dc + 1) * 1024],
                                             in_=wg_b[og * 1024:(og + 1) * 1024,
                                                      dc * 128:(dc + 1) * 128],
                                             transpose=True), "SY", 16)
                            inc(sy.dma_start(out=wut[:, dc * 1024:(dc + 1) * 1024],
                                             in_=wu_b[og * 1024:(og + 1) * 1024,
                                                      dc * 128:(dc + 1) * 128],
                                             transpose=True), "SY", 16)
                            inc(sy.dma_start(out=wdt[:, dc * 1024:(dc + 1) * 1024],
                                             in_=wd_b[:, og * 1024 + dc * 128: og * 1024 + (dc + 1) * 128],
                                             transpose=True), "SY", 16)

                    for tc in range(8):
                        b_, t4 = tc // 4, tc % 4

                        @block.sync
                        def _(sy):
                            sy.wait_ge(sem["MM"], C.get("MM"))
                            inc(sy.dma_start(
                                out=xtc[:, :].rearrange("p (a s) -> p a s", a=8),
                                in_=xt0_dram[b_][:, :, t4 * 512:(t4 + 1) * 512]
                                .rearrange("a p s -> p a s")), "SY", 16)

                        for oc in range(8):
                            @block.tensor
                            def _(t):
                                t.wait_ge(sem["SY"], C.get("SY"))
                                t.wait_ge(sem["VE"], C.get("VE"))
                                for dc in range(8):
                                    mg = t.matmul(PS[0][:, :],
                                                  wgt[:, dc * 1024 + oc * 128: dc * 1024 + oc * 128 + 128],
                                                  xtc[:, dc * 512:(dc + 1) * 512],
                                                  start=(dc == 0), stop=(dc == 7))
                                inc(mg, "MM")
                                for dc in range(8):
                                    mu = t.matmul(PS[1][:, :],
                                                  wut[:, dc * 1024 + oc * 128: dc * 1024 + oc * 128 + 128],
                                                  xtc[:, dc * 512:(dc + 1) * 512],
                                                  start=(dc == 0), stop=(dc == 7))
                                inc(mu, "MM")

                            @block.scalar
                            def _(s):
                                s.wait_ge(sem["MM"], C.get("MM") - 1)
                                s.wait_ge(sem["VE"], C.get("VE"))
                                inc(s.activation(silt[:, :], PS[0][:, :], AF.Silu,
                                                 scale=1.0 / 127.0), "SE")

                            @block.vector
                            def _(v):
                                v.wait_ge(sem["MM"], C.get("MM"))
                                v.wait_ge(sem["SE"], C.get("SE"))
                                inc(v.scalar_tensor_tensor(hglu[:, oc * 512:(oc + 1) * 512],
                                                           PS[1][:, :], 1.0 / 127.0, silt[:, :],
                                                           op0=ALU.mult, op1=ALU.mult), "VE")

                        for ts_ in range(4):
                            @block.tensor
                            def _(t):
                                t.wait_ge(sem["VE"], C.get("VE"))
                                t.wait_ge(sem["SE"], C.get("SE"))
                                for dh in range(2):
                                    for oc in range(8):
                                        m = t.matmul(PS[2 + dh][:, :],
                                                     hglu[:, oc * 512 + ts_ * 128: oc * 512 + ts_ * 128 + 128],
                                                     wdt[:, oc * 1024 + dh * 512: oc * 1024 + (dh + 1) * 512],
                                                     start=(oc == 0), stop=(oc == 7))
                                    inc(m, "MM")

                            @block.scalar
                            def _(s):
                                s.wait_ge(sem["MM"], C.get("MM"))
                                s.wait_ge(sem["SY"], C.get("SY"))
                                s.activation(prow[:, 0:512], PS[2][:, :], AF.Copy, scale=1.0 / 127.0)
                                inc(s.activation(prow[:, 512:1024], PS[3][:, :], AF.Copy,
                                                 scale=1.0 / 127.0), "SE")

                            @block.sync
                            def _(sy):
                                sy.wait_ge(sem["SE"], C.get("SE"))
                                r0 = b_ * S + t4 * 512 + ts_ * 128
                                inc(sy.dma_start(out=ffnp[og, r0:r0 + 128, :], in_=prow[:, :]),
                                    "SY", 16)

            hard_barrier0()

            with ExitStack() as pa:
                xrowf = pa.enter_context(nc.sbuf_tensor("xrowf", [128, 1024], F32))
                arow = pa.enter_context(nc.sbuf_tensor("arow", [128, 1024], BF16))
                p0r = pa.enter_context(nc.sbuf_tensor("p0r", [128, 1024], BF16))
                p1r = pa.enter_context(nc.sbuf_tensor("p1r", [128, 1024], BF16))
                p2r = pa.enter_context(nc.sbuf_tensor("p2r", [128, 1024], BF16))
                x1t = pa.enter_context(nc.sbuf_tensor("x1t", [128, 1024], F32))

                for tt in range(TOK0 // 128):
                    b_ = tt // 16

                    @block.sync
                    def _(sy):
                        sy.wait_ge(sem["VE"], C.get("VE"))
                        r0 = tt * 128
                        inc(sy.dma_start(out=xrowf[:, :],
                                         in_=x_in[b_, (tt % 16) * 128:(tt % 16) * 128 + 128, :]),
                            "SY", 16)
                        inc(sy.dma_start(out=arow[:, :], in_=attn_rows[r0:r0 + 128, :]), "SY", 16)
                        for g_, dst in ((0, p0r), (1, p1r), (2, p2r)):
                            inc(sy.dma_start(out=dst[:, :], in_=ffnp[g_, r0:r0 + 128, :]), "SY", 16)

                    @block.vector
                    def _(v):
                        v.wait_ge(sem["SY"], C.get("SY"))
                        v.wait_ge(sem["GP"], C.get("GP"))
                        v.tensor_tensor(x1t[:, :], xrowf[:, :], arow[:, :], op=ALU.add)
                        v.tensor_tensor(x1t[:, :], x1t[:, :], p0r[:, :], op=ALU.add)
                        v.tensor_tensor(x1t[:, :], x1t[:, :], p1r[:, :], op=ALU.add)
                        inc(v.tensor_tensor(x1t[:, :], x1t[:, :], p2r[:, :], op=ALU.add), "VE")

                    @block.gpsimd
                    def _(g):
                        g.wait_ge(sem["VE"], C.get("VE"))
                        s0 = (tt % 16) * 128
                        dest = s0 // SPC
                        off = dest * (BPC * SPC) + b_ * SPC + (s0 % SPC)
                        inc(g.dma_start(out=a2a_in[off:off + 128, :], in_=x1t[:, :]), "GP", 16)
                        if l0_only:
                            inc(g.dma_start(out=x1_out[tt * 128:(tt + 1) * 128, :], in_=x1t[:, :]),
                                "GP", 16)

            @block.gpsimd
            def _(g):
                g.wait_ge(sem["GP"], C.get("GP"))
                g.wait_ge(sem["SY"], C.get("SY"))
                if l0_only:
                    inc(g.dma_start(out=attn_dbg, in_=attn_rows.ap()), "GP", 16)
                    inc(g.dma_start(out=ffnp_dbg, in_=ffnp.ap()), "GP", 16)
                    inc(g.dma_start(out=xt0_dbg, in_=xt0_dram.ap()), "GP", 16)
                    g.wait_ge(sem["GP"], C.get("GP"))

        if l0_only:
            return nc

        emit_l1(nc, block, sem, C, inc, PS, top, locals())
    return nc

BIGT = 8192.0


def emit_l1(nc, block, sem, C, inc, PS, top, env):
    l1_only = env["l1_only"]
    a2a_out = env["a2a_out"]
    ID, idf_sb, ones_bf, epscol = env["ID"], env["idf_sb"], env["ones_bf"], env["epscol"]
    y_out = env["y_out"]
    xn1tab_h = env["xn1tab"]
    yetab, part1 = env["yetab"], env["part1"]
    cnt_bounce, cnt_all = env["cnt_bounce"], env["cnt_all"]
    moe1, moe2, moe3 = env["moe1"], env["moe2"], env["moe3"]
    moe1_my, moe2_my, moe3_my = env["moe1_my"], env["moe2_my"], env["moe3_my"]
    wqkv_b, sg_b, su_b, sd_b, gw_bd = (env["wqkv_b"], env["sg_b"], env["su_b"],
                                       env["sd_b"], env["gw_bd"])

    def vw(v):
        v.wait_ge(sem["VE"], C.get("VE"))

    def hard_barrier():
        for engname in ("tensor", "vector", "scalar", "sync", "gpsimd"):
            def _bar(eng):
                for s_ in ("SY", "GP", "MM", "VE", "SE", "CC"):
                    eng.wait_ge(sem[s_], C.get(s_))
            getattr(block, engname)(_bar)

    if not l1_only:
        a2a_in = env["a2a_in"]

        @block.gpsimd
        def _(g):
            g.wait_ge(sem["GP"], C.get("GP"))
            with allow_ici_all_to_all():
                inc(g.collective_compute("AllToAll", ALU.bypass,
                                         replica_groups=[list(range(NCORES))],
                                         ins=[a2a_in.ap()], outs=[a2a_out]), "CC")
            g.wait_ge(sem["CC"], C.get("CC"))

    @block.gpsimd
    def _(g):
        inc(g.dma_start(out=wqkv_b.ap(), in_=env["wqkv_in"]), "GP", 16)
        inc(g.dma_start(out=sg_b.ap(), in_=env["sg_in"]), "GP", 16)
        inc(g.dma_start(out=su_b.ap(), in_=env["su_in"]), "GP", 16)
        inc(g.dma_start(out=sd_b.ap(), in_=env["sd_in"]), "GP", 16)
        inc(g.dma_start(out=gw_bd.ap(), in_=env["gw_in"]), "GP", 16)
        inc(g.dma_start(out=moe1_my.ap(), in_=env["w1_my"]), "GP", 16)
        inc(g.dma_start(out=moe3_my.ap(), in_=env["w3_my"]), "GP", 16)
        inc(g.dma_start(out=moe2_my.ap(), in_=env["w2_my"]), "GP", 16)
        g.wait_ge(sem["GP"], C.get("GP"))
        for my, full in ((moe1_my, moe1), (moe3_my, moe3), (moe2_my, moe2)):
            inc(g.collective_compute(
                "AllGather", ALU.bypass, replica_groups=[list(range(NCORES))],
                ins=[my.ap().rearrange("e a b -> (e a) b")],
                outs=[full.ap().rearrange("e a b -> (e a) b")]), "CC")

    l1c = ExitStack()
    xn1t = l1c.enter_context(nc.sbuf_tensor("xn1t", [128, 8 * TOK1], BF16))
    onesf = l1c.enter_context(nc.sbuf_tensor("onesf", [1, 128], F32))
    onescf = l1c.enter_context(nc.sbuf_tensor("onescf", [128, 1], F32))
    utile_s = l1c.enter_context(nc.sbuf_tensor("utile_s", [128, 128], F32))
    uplain_s = l1c.enter_context(nc.sbuf_tensor("uplain_s", [128, 128], F32))
    ebm_s = l1c.enter_context(nc.sbuf_tensor("ebm_s", [128, 16], F32))
    ebt_s = l1c.enter_context(nc.sbuf_tensor("ebt_s", [16, 128], F32))
    u1_s = l1c.enter_context(nc.sbuf_tensor("u1_s", [128, 16], F32))
    iota_s = l1c.enter_context(nc.sbuf_tensor("iota_s", [128, 1], F32))
    bd16_s = l1c.enter_context(nc.sbuf_tensor("bd16_s", [128, 512], BF16))
    ebias_bc = l1c.enter_context(nc.sbuf_tensor("ebias_bc", [128, 64], F32))
    e192p1 = l1c.enter_context(nc.sbuf_tensor("e192p1", [128, 64], F32))
    gammac = l1c.enter_context(nc.sbuf_tensor("gammac", [128, 16], F32))
    c2c = l1c.enter_context(nc.sbuf_tensor("c2c", [128, 16], F32))
    qkb2 = l1c.enter_context(nc.sbuf_tensor("qkb2", [128, 16], F32))
    vbias = l1c.enter_context(nc.sbuf_tensor("vbias", [128, 1024], F32))
    gwT = l1c.enter_context(nc.sbuf_tensor("gwT", [128, 512], BF16))
    destbuf = l1c.enter_context(nc.sbuf_tensor("destbuf", [128, 256], I32))
    gidxT0 = l1c.enter_context(nc.sbuf_tensor("gidxT0", [128, 64], I32))
    gidxT1 = l1c.enter_context(nc.sbuf_tensor("gidxT1", [128, 64], I32))
    zrow = l1c.enter_context(nc.sbuf_tensor("zrow", [1, D + E], BF16))
    srow = l1c.enter_context(nc.sbuf_tensor("srow", [16, 128], F32))
    brow = l1c.enter_context(nc.sbuf_tensor("brow", [1, 1120], F32))

    @block.sync
    def _(sy):
        inc(sy.dma_start(out=utile_s[:, :], in_=env["u_tile"]), "SY", 16)
        inc(sy.dma_start(out=uplain_s[:, :], in_=env["u_plain"]), "SY", 16)
        inc(sy.dma_start(out=ebm_s[:, :], in_=env["eb_mat"]), "SY", 16)
        inc(sy.dma_start(out=ebt_s[:, :], in_=env["ebt_mat"]), "SY", 16)
        inc(sy.dma_start(out=u1_s[:, :], in_=env["u1_mine"]), "SY", 16)
        inc(sy.dma_start(out=iota_s[:, :], in_=env["iota_col"]), "SY", 16)
        inc(sy.dma_start(out=bd16_s[:, :], in_=env["bd16_mask"]), "SY", 16)
        inc(sy.dma_start(out=brow[0:1, 0:64], in_=env["e192_row"]), "SY", 16)
        inc(sy.dma_start(out=srow[:, :],
                         in_=env["bqkv_in"][0:1, :]
                         .rearrange("o (h c) -> (o h) c", h=16)[:, 0:128]), "SY", 16)

    @block.vector
    def _(v):
        v.wait_ge(sem["SY"], C.get("SY"))
        v.memset(onesf[:, :], 1.0)
        v.memset(onescf[:, :], 1.0)
        inc(v.memset(zrow[:, :], 0.0), "VE")

    @block.gpsimd
    def _(g):
        g.wait_ge(sem["VE"], C.get("VE"))
        inc(g.dma_start(out=xn1tab_h.ap()[0:1, :], in_=zrow[0:1, :]), "GP", 16)
        inc(g.dma_start(out=yetab.ap()[0:1, :], in_=zrow[0:1, 0:D]), "GP", 16)

    @block.tensor
    def _(t):
        t.wait_ge(sem["VE"], C.get("VE"))
        inc(t.matmul(PS[5][:, 0:64], onesf[0:1, :], brow[0:1, 0:64],
                     start=True, stop=True), "MM")
        inc(t.transpose(PS[6][0:128, 0:16], srow[:, :], idf_sb[0:16, 0:16]), "MM")

    @block.vector
    def _(v):
        v.wait_ge(sem["MM"], C.get("MM"))
        inc(v.tensor_copy(e192p1[:, :], PS[5][:, 0:64]), "VE")
        inc(v.tensor_copy(qkb2[:, :], PS[6][0:128, 0:16]), "VE")

    @block.sync
    def _(sy):
        sy.wait_ge(sem["MM"], C.get("MM"))
        inc(sy.dma_start(out=brow[0:1, 0:64], in_=env["ebias_in"]), "SY", 16)
        inc(sy.dma_start(out=brow[0:1, 64:80], in_=env["gamma_in"]), "SY", 16)
        inc(sy.dma_start(out=brow[0:1, 80:96], in_=env["beta_in"]), "SY", 16)
        inc(sy.dma_start(
            out=brow[0:1, 96:608].rearrange("o (h c) -> o h c", h=8),
            in_=env["bqkv_in"][0:1, :]
            .rearrange("o (h c) -> o h c", h=16)[:, 0:8, 128:192]), "SY", 16)
        inc(sy.dma_start(
            out=brow[0:1, 608:1120].rearrange("o (h c) -> o h c", h=8),
            in_=env["bqkv_in"][0:1, :]
            .rearrange("o (h c) -> o h c", h=16)[:, 8:16, 128:192]), "SY", 16)
        sy.wait_ge(sem["GP"], C.get("GP"))
        for dc in range(8):
            inc(sy.dma_start(out=gwT[:, dc * 64:(dc + 1) * 64],
                             in_=gw_bd.ap()[:, dc * 128:(dc + 1) * 128], transpose=True),
                "SY", 16)

    @block.tensor
    def _(t):
        t.wait_ge(sem["SY"], C.get("SY"))
        t.wait_ge(sem["VE"], C.get("VE"))
        inc(t.matmul(PS[5][:, 0:64], onesf[0:1, :], brow[0:1, 0:64],
                     start=True, stop=True), "MM")
        inc(t.matmul(PS[6][:, 0:16], onesf[0:1, :], brow[0:1, 64:80],
                     start=True, stop=True), "MM")
        inc(t.matmul(PS[6][:, 16:32], onesf[0:1, :], brow[0:1, 80:96],
                     start=True, stop=True), "MM")
        inc(t.matmul(PS[7][:, 0:512], onesf[0:1, :], brow[0:1, 96:608],
                     start=True, stop=True), "MM")
        inc(t.matmul(PS[4][:, 0:512], onesf[0:1, :], brow[0:1, 608:1120],
                     start=True, stop=True), "MM")

    @block.vector
    def _(v):
        v.wait_ge(sem["MM"], C.get("MM"))
        inc(v.tensor_copy(ebias_bc[:, :], PS[5][:, 0:64]), "VE")
        inc(v.tensor_copy(gammac[:, :], PS[6][:, 0:16]), "VE")
        inc(v.tensor_copy(vbias[:, 0:512], PS[7][:, 0:512]), "VE")
        inc(v.tensor_copy(vbias[:, 512:1024], PS[4][:, 0:512]), "VE")
        vw(v)
        inc(v.tensor_scalar(c2c[:, :], gammac[:, :], 0.5, None, op0=ALU.mult), "VE")
        vw(v)
        inc(v.tensor_tensor(c2c[:, :], c2c[:, :], PS[6][:, 16:32], op=ALU.add), "VE")

    p45 = ExitStack()
    lnrow = p45.enter_context(nc.sbuf_tensor("l1_lnrow", [128, 1024], F32))
    lnsc = p45.enter_context(nc.sbuf_tensor("l1_lnsc", [128, 8], F32))
    xnrow = p45.enter_context(nc.sbuf_tensor("l1_xnrow", [128, 1024], BF16))
    x1v = a2a_out.rearrange("(b s) d -> b s d", b=B)

    for tt in range(32):
        @block.sync
        def _(sy):
            sy.wait_ge(sem["MM"], C.get("MM"))
            for ss in range(8):
                inc(sy.dma_start(out=lnrow[ss * 16:(ss + 1) * 16, :],
                                 in_=x1v[:, 8 * tt + ss, :]), "SY", 16)

        @block.scalar
        def _(s):
            s.wait_ge(sem["SY"], C.get("SY"))
            s.activation(xnrow[:, :], lnrow[:, :], AF.Copy, scale=1.0 / D,
                         accum_out=lnsc[:, 1:2])
            inc(s.activation(xnrow[:, :], lnrow[:, :], AF.Square,
                             scale=1.0 / float(np.sqrt(D)), accum_out=lnsc[:, 2:3]), "SE")

        @block.vector
        def _(v):
            v.wait_ge(sem["SE"], C.get("SE"))
            inc(v.tensor_tensor(lnsc[:, 3:4], lnsc[:, 1:2], lnsc[:, 1:2], op=ALU.mult), "VE")
            vw(v)
            inc(v.tensor_tensor(lnsc[:, 5:6], lnsc[:, 2:3], lnsc[:, 3:4], op=ALU.subtract), "VE")

        @block.scalar
        def _(s):
            s.wait_ge(sem["VE"], C.get("VE"))
            inc(s.activation(lnsc[:, 7:8], lnsc[:, 5:6], AF.Sqrt, bias=epscol[:, 0:1]), "SE")

        @block.vector
        def _(v):
            v.wait_ge(sem["SE"], C.get("SE"))
            inc(v.reciprocal(lnsc[:, 0:1], lnsc[:, 7:8]), "VE")
            vw(v)
            inc(v.tensor_scalar(xnrow[:, :], lnrow[:, :], lnsc[:, 1:2], lnsc[:, 0:1],
                                op0=ALU.subtract, op1=ALU.mult), "VE")

        @block.gpsimd
        def _(g):
            g.wait_ge(sem["VE"], C.get("VE"))
            inc(g.dma_start(out=xn1tab_h.ap()[1 + tt * 128: 1 + (tt + 1) * 128, 0:D],
                            in_=xnrow[:, :]), "GP", 16)

        @block.tensor
        def _(t):
            t.wait_ge(sem["VE"], C.get("VE"))
            for dc in range(8):
                last = t.transpose(PS[7][:, :].bitcast(BF16)[:, dc * 128:(dc + 1) * 128],
                                   xnrow[:, dc * 128:(dc + 1) * 128], ID)
            inc(last, "MM")

        @block.vector
        def _(v):
            v.wait_ge(sem["MM"], C.get("MM"))
            inc(v.tensor_copy(
                xn1t[:, :].rearrange("p (a s) -> p a s", a=8)[:, :, tt * 128:(tt + 1) * 128],
                PS[7][:, :].bitcast(BF16).rearrange("p (a s) -> p a s", a=8)), "VE")

    sco = p45.enter_context(nc.sbuf_tensor("sco", [128, 64], F32))
    sfc = p45.enter_context(nc.sbuf_tensor("sfc", [128, 64], F32))
    g8buf = p45.enter_context(nc.sbuf_tensor("g8buf", [128, 64], F32))
    gs16 = p45.enter_context(nc.sbuf_tensor("gs16", [128, 16], F32))
    gmi = p45.enter_context(nc.sbuf_tensor("gmi", [128, 8], I32))
    gmf = p45.enter_context(nc.sbuf_tensor("gmf", [128, 8], F32))
    tmp8 = p45.enter_context(nc.sbuf_tensor("tmp8", [128, 64], F32))
    t8v = p45.enter_context(nc.sbuf_tensor("t8v", [128, 8], F32))
    zap = p45.enter_context(nc.sbuf_tensor("zap", [128, 64], F32))
    ai32 = p45.enter_context(nc.sbuf_tensor("ai32", [128, 64], I32))
    afbuf = p45.enter_context(nc.sbuf_tensor("afbuf", [128, 2048], F32))
    wnb = p45.enter_context(nc.sbuf_tensor("wnb", [128, 64], BF16))
    wnf = p45.enter_context(nc.sbuf_tensor("wnf", [128, 64], F32))

    for tt in range(32):
        @block.tensor
        def _(t):
            t.wait_ge(sem["SE"], C.get("SE"))
            t.wait_ge(sem["VE"], C.get("VE"))
            for dc in range(8):
                mm = t.matmul(PS[5][:, 0:64],
                              xn1t[:, dc * TOK1 + tt * 128: dc * TOK1 + (tt + 1) * 128],
                              gwT[:, dc * 64:(dc + 1) * 64],
                              start=(dc == 0), stop=(dc == 7))
            inc(mm, "MM")

        @block.scalar
        def _(s):
            s.wait_ge(sem["MM"], C.get("MM"))
            s.wait_ge(sem["VE"], C.get("VE"))
            inc(s.activation(sco[:, :], PS[5][:, 0:64], AF.Sigmoid), "SE")

        @block.vector
        def _(v):
            v.wait_ge(sem["SE"], C.get("SE"))
            inc(v.tensor_tensor(sfc[:, :], sco[:, :], ebias_bc[:, :], op=ALU.add), "VE")
            vw(v)
            for g_ in range(8):
                last = v.max(out=g8buf[:, 8 * g_:8 * g_ + 8], in_=sfc[:, 8 * g_:8 * g_ + 8])
            inc(last, "VE")
            vw(v)
            for g_ in range(8):
                last = v.reduce_sum(gs16[:, g_:g_ + 1], g8buf[:, 8 * g_:8 * g_ + 2], axis=AX)
            inc(last, "VE")
            vw(v)
            inc(v.max(out=gs16[:, 8:16], in_=gs16[:, 0:8]), "VE")
            vw(v)
            for g_ in range(8):
                last = v.tensor_tensor(gmi[:, g_:g_ + 1], gs16[:, g_:g_ + 1],
                                       gs16[:, 11:12], op=ALU.is_ge)
            inc(last, "VE")
            vw(v)
            inc(v.tensor_copy(gmf[:, :], gmi[:, :]), "VE")
            vw(v)
            for g_ in range(8):
                last = v.tensor_scalar(tmp8[:, 8 * g_:8 * g_ + 8], sfc[:, 8 * g_:8 * g_ + 8],
                                       gmf[:, g_:g_ + 1], None, op0=ALU.mult)
            inc(last, "VE")
            vw(v)
            inc(v.max(out=t8v[:, :], in_=tmp8[:, :]), "VE")
            vw(v)
            inc(v.match_replace(out=zap[:, :], in_to_replace=t8v[:, :],
                                in_values=tmp8[:, :], imm_value=NEG), "VE")
            vw(v)
            inc(v.tensor_scalar(ai32[:, :], zap[:, :], -1e29, None, op0=ALU.is_le), "VE")
            vw(v)
            inc(v.tensor_copy(afbuf[:, tt * 64:(tt + 1) * 64], ai32[:, :]), "VE")
            vw(v)
            inc(v.tensor_tensor(wnf[:, :], afbuf[:, tt * 64:(tt + 1) * 64], sco[:, :],
                                op=ALU.mult), "VE")
            vw(v)
            inc(v.reduce_sum(lnsc[:, 4:5], wnf[:, :], axis=AX), "VE")
            vw(v)
            inc(v.tensor_scalar(lnsc[:, 5:6], lnsc[:, 4:5], 1e-20, None, op0=ALU.add), "VE")
            vw(v)
            inc(v.reciprocal(lnsc[:, 6:7], lnsc[:, 5:6]), "VE")
            vw(v)
            inc(v.tensor_scalar(wnb[:, :], wnf[:, :], lnsc[:, 6:7], 2.5,
                                op0=ALU.mult, op1=ALU.mult), "VE")

        @block.gpsimd
        def _(g):
            g.wait_ge(sem["VE"], C.get("VE"))
            inc(g.dma_start(out=xn1tab_h.ap()[1 + tt * 128: 1 + (tt + 1) * 128, D:D + E],
                            in_=wnb[:, :]), "GP", 16)

        @block.tensor
        def _(t):
            t.wait_ge(sem["VE"], C.get("VE"))
            inc(t.matmul(PS[6][0:16, 0:64], ebm_s[:, :], afbuf[:, tt * 64:(tt + 1) * 64],
                         start=(tt == 0), stop=(tt == 31)), "MM")

    cnt16 = p45.enter_context(nc.sbuf_tensor("cnt16", [16, 64], F32))
    gb16 = p45.enter_context(nc.sbuf_tensor("gb16", [16, 64], F32))
    gbr = p45.enter_context(nc.sbuf_tensor("gbr", [16, 64], F32))
    runb = p45.enter_context(nc.sbuf_tensor("runb", [16, 64], F32))
    runk = p45.enter_context(nc.sbuf_tensor("runk", [1, 64], F32))
    cntall = p45.enter_context(nc.sbuf_tensor("cntall", [128, 64], F32))

    @block.vector
    def _(v):
        v.wait_ge(sem["MM"], C.get("MM"))
        inc(v.tensor_copy(cnt16[:, :], PS[6][0:16, 0:64]), "VE")
        v.memset(runb[:, :], 0.0)
        inc(v.memset(runk[:, :], 0.0), "VE")

    @block.gpsimd
    def _(g):
        g.wait_ge(sem["VE"], C.get("VE"))
        inc(g.dma_start(out=cnt_bounce.ap(), in_=cnt16[:, :]), "GP", 16)
        g.wait_ge(sem["GP"], C.get("GP"))
        inc(g.collective_compute("AllGather", ALU.bypass,
                                 replica_groups=[list(range(NCORES))],
                                 ins=[cnt_bounce.ap()], outs=[cnt_all.ap()]), "CC")
        g.wait_ge(sem["CC"], C.get("CC"))
        inc(g.dma_start(out=cntall[:, :], in_=cnt_all.ap()), "GP", 16)

    @block.tensor
    def _(t):
        t.wait_ge(sem["GP"], C.get("GP"))
        inc(t.matmul(PS[5][0:16, 0:64], u1_s[:, :], cntall[:, :], start=True, stop=True),
            "MM")

    @block.vector
    def _(v):
        v.wait_ge(sem["MM"], C.get("MM"))
        inc(v.tensor_copy(gb16[:, :], PS[5][0:16, 0:64]), "VE")

    kint = p45.enter_context(nc.sbuf_tensor("kint", [128, 192], I32))
    keepf = p45.enter_context(nc.sbuf_tensor("keepf", [128, 64], F32))
    tmpc = p45.enter_context(nc.sbuf_tensor("tmpc", [128, 64], F32))
    fcol = p45.enter_context(nc.sbuf_tensor("fcol", [128, 1], F32))
    sval = p45.enter_context(nc.sbuf_tensor("sval", [128, 64], F32))
    evT = p45.enter_context(nc.sbuf_tensor("evT", [64, TOK1], F32))
    d8 = p45.enter_context(nc.sbuf_tensor("d8", [128, 8], F32))
    d8b = p45.enter_context(nc.sbuf_tensor("d8b", [128, 8], F32))
    m8i = p45.enter_context(nc.sbuf_tensor("m8i", [128, 8], I32))
    m8f = p45.enter_context(nc.sbuf_tensor("m8f", [128, 8], F32))

    for tt in range(32):
        @block.vector
        def _(v):
            vw(v)
            inc(v.tensor_tensor(gbr[:, :], gb16[:, :], runb[:, :], op=ALU.add), "VE")
            vw(v)

        @block.tensor
        def _(t):
            t.wait_ge(sem["VE"], C.get("VE"))
            af = afbuf[:, tt * 64:(tt + 1) * 64]
            t.matmul(PS[4][:, 0:64], utile_s[:, :], af, start=True, stop=False)
            inc(t.matmul(PS[4][:, 0:64], ebt_s[:, :], gbr[:, :], start=False, stop=True),
                "MM")
            inc(t.matmul(PS[2][0:16, 0:64], ebm_s[:, :], af, start=True, stop=True), "MM")

        @block.vector
        def _(v):
            v.wait_ge(sem["MM"], C.get("MM"))
            inc(v.tensor_scalar(kint[:, 0:64], PS[4][:, 0:64], float(CAP), None,
                                op0=ALU.is_lt), "VE")
            vw(v)
            inc(v.tensor_copy(keepf[:, :], kint[:, 0:64]), "VE")
            vw(v)
            inc(v.tensor_tensor(keepf[:, :], keepf[:, :],
                                afbuf[:, tt * 64:(tt + 1) * 64], op=ALU.mult), "VE")
            vw(v)
            inc(v.tensor_tensor(runb[:, :], runb[:, :], PS[2][0:16, 0:64], op=ALU.add), "VE")
            vw(v)

        @block.tensor
        def _(t):
            t.wait_ge(sem["VE"], C.get("VE"))
            t.matmul(PS[3][:, 0:64], uplain_s[:, :], keepf[:, :], start=True, stop=False)
            inc(t.matmul(PS[3][:, 0:64], onesf[0:1, :], runk[0:1, :], start=False,
                         stop=True), "MM")
            inc(t.matmul(PS[1][0:1, 0:64], onescf[:, :], keepf[:, :], start=True,
                         stop=True), "MM")

        @block.vector
        def _(v):
            v.wait_ge(sem["MM"], C.get("MM"))
            inc(v.tensor_scalar(fcol[:, :], iota_s[:, :], -1.0, BIGT - 1.0 - 128.0 * tt,
                                op0=ALU.mult, op1=ALU.add), "VE")
            vw(v)
            inc(v.tensor_scalar(sval[:, :], keepf[:, :], fcol[:, 0:1], None,
                                op0=ALU.mult), "VE")
            inc(v.scalar_tensor_tensor(tmpc[:, :], PS[3][:, 0:64], 1.0, e192p1[:, :],
                                       op0=ALU.mult, op1=ALU.add), "VE")
            vw(v)
            inc(v.tensor_scalar(tmpc[:, :], tmpc[:, :], -1.0, EBIG,
                                op0=ALU.mult, op1=ALU.add), "VE")
            vw(v)
            inc(v.tensor_tensor(tmpc[:, :], tmpc[:, :], keepf[:, :], op=ALU.mult), "VE")
            vw(v)
            inc(v.max(out=d8[:, :], in_=tmpc[:, :]), "VE")
            vw(v)
            inc(v.tensor_scalar(d8b[:, :], d8[:, :], -1.0, EBIG, op0=ALU.mult,
                                op1=ALU.add), "VE")
            inc(v.tensor_scalar(m8i[:, :], d8[:, :], 0.0, None, op0=ALU.is_gt), "VE")
            vw(v)
            inc(v.tensor_copy(m8f[:, :], m8i[:, :]), "VE")
            vw(v)
            inc(v.tensor_tensor(d8b[:, :], d8b[:, :], m8f[:, :], op=ALU.mult), "VE")
            vw(v)
            inc(v.tensor_copy(destbuf[:, tt * 8:(tt + 1) * 8], d8b[:, :]), "VE")
            inc(v.tensor_tensor(runk[0:1, :], runk[0:1, :], PS[1][0:1, 0:64],
                                op=ALU.add), "VE")
            vw(v)

        @block.tensor
        def _(t):
            t.wait_ge(sem["VE"], C.get("VE"))
            inc(t.transpose(PS[7][0:64, 0:128], sval[:, :], idf_sb[:, :]), "MM")

        @block.vector
        def _(v):
            v.wait_ge(sem["MM"], C.get("MM"))
            inc(v.tensor_copy(evT[:, tt * 128:(tt + 1) * 128], PS[7][0:64, 0:128]), "VE")

    cand = p45.enter_context(nc.sbuf_tensor("cand", [64, 320], F32))
    gext = p45.enter_context(nc.sbuf_tensor("gext", [64, 192], F32))
    gfl = p45.enter_context(nc.sbuf_tensor("gfl", [64, 384], F32))

    @block.vector
    def _(v):
        vw(v)
        for blk in range(8):
            for r in range(5):
                inc(v.max(out=cand[:, blk * 40 + r * 8: blk * 40 + (r + 1) * 8],
                          in_=evT[:, blk * 512:(blk + 1) * 512]), "VE")
                vw(v)
                if r < 4:
                    inc(v.match_replace(out=evT[:, blk * 512:(blk + 1) * 512],
                                        in_to_replace=cand[:, blk * 40 + r * 8: blk * 40 + (r + 1) * 8],
                                        in_values=evT[:, blk * 512:(blk + 1) * 512],
                                        imm_value=0.0), "VE")
                    vw(v)
        for r in range(24):
            inc(v.max(out=gext[:, r * 8:(r + 1) * 8], in_=cand[:, :]), "VE")
            vw(v)
            if r < 23:
                inc(v.match_replace(out=cand[:, :], in_to_replace=gext[:, r * 8:(r + 1) * 8],
                                    in_values=cand[:, :], imm_value=0.0), "VE")
                vw(v)
        inc(v.tensor_scalar(gfl[:, 0:192], gext[:, :], -1.0, BIGT, op0=ALU.mult,
                            op1=ALU.add), "VE")
        inc(v.tensor_scalar(kint[0:64, 0:192], gext[:, :], 0.0, None, op0=ALU.is_gt), "VE")
        vw(v)
        inc(v.tensor_copy(gfl[:, 192:384], kint[0:64, 0:192]), "VE")
        vw(v)
        inc(v.tensor_tensor(gfl[:, 0:192], gfl[:, 0:192], gfl[:, 192:384], op=ALU.mult),
            "VE")
        vw(v)

    @block.tensor
    def _(t):
        t.wait_ge(sem["VE"], C.get("VE"))
        inc(t.transpose(PS[7][0:128, 0:64], gfl[:, 0:128], idf_sb[0:64, 0:64]), "MM")
        inc(t.transpose(PS[6][0:64, 0:64], gfl[:, 128:192], idf_sb[0:64, 0:64]), "MM")

    @block.vector
    def _(v):
        v.wait_ge(sem["MM"], C.get("MM"))
        inc(v.tensor_copy(gidxT0[:, :], PS[7][0:128, 0:64]), "VE")
        v.memset(gidxT1[:, :], 0)
        vw(v)
        inc(v.tensor_copy(gidxT1[0:64, :], PS[6][0:64, 0:64]), "VE")

    if l1_only:
        @block.gpsimd
        def _(g):
            g.wait_ge(sem["VE"], C.get("VE"))
            inc(g.dma_start(out=env["dbg_af"], in_=afbuf[:, :]), "GP", 16)
            inc(g.dma_start(out=env["dbg_cnt"][0:16, 0:64], in_=cnt16[:, :]), "GP", 16)
            inc(g.dma_start(out=env["dbg_cnt"][16:32, 64:128], in_=gb16[:, :]), "GP", 16)
            inc(g.dma_start(out=env["dbg_cnt"][0:128, 128:192], in_=cntall[:, :]), "GP", 16)

    hard_barrier()
    p45.close()

    p6 = ExitStack()
    qkwT = p6.enter_context(nc.sbuf_tensor("qkwT", [128, NH * 8 * 128], BF16))
    vwT = p6.enter_context(nc.sbuf_tensor("vwT", [128, 8 * 1024], BF16))
    qT = p6.enter_context(nc.sbuf_tensor("qT", [64, NH * 512], BF16))
    kT = p6.enter_context(nc.sbuf_tensor("kT", [64, NH * 512], BF16))
    vtok = p6.enter_context(nc.sbuf_tensor("vtok", [128, 4 * 1024], BF16))
    xb6 = p6.enter_context(nc.sbuf_tensor("xb6", [128, 512], BF16))
    x26 = p6.enter_context(nc.sbuf_tensor("x26", [128, 512], BF16))
    i16 = p6.enter_context(nc.sbuf_tensor("i16", [128, 512], BF16))
    i26 = p6.enter_context(nc.sbuf_tensor("i26", [128, 512], BF16))
    probsb = p6.enter_context(nc.sbuf_tensor("probsb", [128, 512], BF16))
    a1rows = p6.enter_context(nc.sbuf_tensor("a1rows", [128, 4 * 1024], BF16))

    @block.sync
    def _(sy):
        sy.wait_ge(sem["GP"], C.get("GP"))
        for h in range(NH):
            for dc in range(8):
                inc(sy.dma_start(out=qkwT[:, (h * 8 + dc) * 128:(h * 8 + dc + 1) * 128],
                                 in_=wqkv_b.ap()[192 * h:192 * h + 128,
                                                 dc * 128:(dc + 1) * 128],
                                 transpose=True), "SY", 16)
        for h in range(NH):
            for dc in range(8):
                inc(sy.dma_start(out=vwT[:, dc * 1024 + h * 64: dc * 1024 + (h + 1) * 64],
                                 in_=wqkv_b.ap()[192 * h + 128:192 * h + 192,
                                                 dc * 128:(dc + 1) * 128],
                                 transpose=True), "SY", 16)

    for ck in range(8):
        for h in range(NH):
            bank = 5 + (h % 2)

            @block.tensor
            def _(t):
                if h == 0:
                    t.wait_ge(sem["SY"], C.get("SY"))
                t.wait_ge(sem["VE"], C.get("VE"))
                for dc in range(8):
                    mm = t.matmul(PS[bank][:, :],
                                  qkwT[:, (h * 8 + dc) * 128:(h * 8 + dc + 1) * 128],
                                  xn1t[:, dc * TOK1 + ck * 512: dc * TOK1 + (ck + 1) * 512],
                                  start=(dc == 0), stop=(dc == 7))
                inc(mm, "MM")

            @block.vector
            def _(v):
                v.wait_ge(sem["MM"], C.get("MM"))
                v.tensor_scalar(qT[:, h * 512:(h + 1) * 512], PS[bank][0:64, :],
                                qkb2[0:64, h:h + 1], 0.125, op0=ALU.add, op1=ALU.mult)
                inc(v.tensor_scalar(kT[:, h * 512:(h + 1) * 512], PS[bank][64:128, :],
                                    qkb2[64:128, h:h + 1], None, op0=ALU.add), "VE")

        for ts_ in range(4):
            @block.tensor
            def _(t):
                t.wait_ge(sem["SE"], C.get("SE"))
                t.wait_ge(sem["VE"], C.get("VE"))
                for hg in range(2):
                    for dc in range(8):
                        mm = t.matmul(PS[5 + hg][:, :],
                                      xn1t[:, dc * TOK1 + ck * 512 + ts_ * 128:
                                           dc * TOK1 + ck * 512 + (ts_ + 1) * 128],
                                      vwT[:, dc * 1024 + hg * 512: dc * 1024 + (hg + 1) * 512],
                                      start=(dc == 0), stop=(dc == 7))
                    inc(mm, "MM")

            @block.vector
            def _(v):
                for hg in range(2):
                    v.wait_ge(sem["MM"], C.get("MM") - (1 - hg))
                    inc(v.scalar_tensor_tensor(
                        vtok[:, ts_ * 1024 + hg * 512: ts_ * 1024 + (hg + 1) * 512],
                        PS[5 + hg][:, :], 1.0, vbias[:, hg * 512:(hg + 1) * 512],
                        op0=ALU.mult, op1=ALU.add), "VE")

        for hg in range(2):
            for hh in range(8):
                h = hg * 8 + hh

                @block.tensor
                def _(t):
                    t.wait_ge(sem["SE"], C.get("SE"))
                    t.wait_ge(sem["VE"], C.get("VE"))
                    for spb in range(4):
                        mm = t.matmul(PS[0][:, spb * 128:(spb + 1) * 128],
                                      kT[:, h * 512 + spb * 128: h * 512 + (spb + 1) * 128],
                                      qT[:, h * 512 + spb * 128: h * 512 + (spb + 1) * 128],
                                      start=True, stop=True)
                    inc(mm, "MM")

                @block.scalar
                def _(s):
                    s.wait_ge(sem["MM"], C.get("MM"))
                    s.wait_ge(sem["VE"], C.get("VE"))
                    inc(s.copy(xb6[:, :], PS[0][:, :]), "SE")

                @block.vector
                def _(v):
                    v.wait_ge(sem["SE"], C.get("SE"))
                    v.wait_ge(sem["MM"], C.get("MM"))
                    v.tensor_tensor(x26[:, :], xb6[:, :], xb6[:, :], op=ALU.mult)
                    v.tensor_scalar(i16[:, :], x26[:, :], 1.0 / 480.0, -1.0 / 48.0,
                                    op0=ALU.mult, op1=ALU.add)
                    v.tensor_tensor(i26[:, :], i16[:, :], x26[:, :], op=ALU.mult)
                    v.scalar_tensor_tensor(i16[:, :], i26[:, :], 0.25, xb6[:, :],
                                           op0=ALU.add, op1=ALU.mult)
                    v.tensor_scalar(i26[:, :], i16[:, :], gammac[:, h:h + 1],
                                    c2c[:, h:h + 1], op0=ALU.mult, op1=ALU.add)
                    inc(v.tensor_tensor(probsb[:, :], i26[:, :], bd16_s[:, :],
                                        op=ALU.mult), "VE")

                @block.tensor
                def _(t):
                    t.wait_ge(sem["VE"], C.get("VE"))
                    for spb in range(4):
                        mm = t.matmul(PS[1 + spb][:, hh * 64:(hh + 1) * 64],
                                      probsb[:, spb * 128:(spb + 1) * 128],
                                      vtok[:, spb * 1024 + h * 64: spb * 1024 + (h + 1) * 64],
                                      start=True, stop=True)
                    inc(mm, "MM")

            @block.vector
            def _(v):
                v.wait_ge(sem["MM"], C.get("MM"))
                for spb in range(4):
                    last = v.tensor_copy(
                        a1rows[:, spb * 1024 + hg * 512: spb * 1024 + (hg + 1) * 512],
                        PS[1 + spb][:, :])
                inc(last, "VE")

        @block.gpsimd
        def _(g):
            g.wait_ge(sem["VE"], C.get("VE"))
            for ts_ in range(4):
                r0 = ck * 512 + ts_ * 128
                inc(g.dma_start(out=part1.ap()[r0:r0 + 128, :],
                                in_=a1rows[:, ts_ * 1024:(ts_ + 1) * 1024]), "GP", 16)

    hard_barrier()
    p6.close()

    p6b = ExitStack()
    sgT = p6b.enter_context(nc.sbuf_tensor("sgT", [128, 8 * 1024], BF16))
    suT = p6b.enter_context(nc.sbuf_tensor("suT", [128, 8 * 1024], BF16))
    sdT = p6b.enter_context(nc.sbuf_tensor("sdT", [128, 8 * 1024], BF16))
    hh6 = p6b.enter_context(nc.sbuf_tensor("hh6", [128, 8 * 512], BF16))
    sil6 = p6b.enter_context(nc.sbuf_tensor("sil6", [128, 512], BF16))
    prow6 = p6b.enter_context(nc.sbuf_tensor("prow6", [128, 1024], BF16))
    a1ld = p6b.enter_context(nc.sbuf_tensor("a1ld", [128, 4 * 1024], BF16))

    @block.sync
    def _(sy):
        sy.wait_ge(sem["MM"], C.get("MM"))
        sy.wait_ge(sem["GP"], C.get("GP"))
        for (dstw, srcw) in ((sgT, sg_b), (suT, su_b), (sdT, sd_b)):
            for dc in range(8):
                inc(sy.dma_start(out=dstw[:, dc * 1024:(dc + 1) * 1024],
                                 in_=srcw.ap()[:, dc * 128:(dc + 1) * 128],
                                 transpose=True), "SY", 16)

    for ck in range(8):
        @block.sync
        def _(sy):
            sy.wait_ge(sem["VE"], C.get("VE"))
            for ts_ in range(4):
                r0 = ck * 512 + ts_ * 128
                inc(sy.dma_start(out=a1ld[:, ts_ * 1024:(ts_ + 1) * 1024],
                                 in_=part1.ap()[r0:r0 + 128, :]), "SY", 16)

        for oc in range(8):
            @block.tensor
            def _(t):
                t.wait_ge(sem["SE"], C.get("SE"))
                t.wait_ge(sem["VE"], C.get("VE"))
                if oc == 0:
                    t.wait_ge(sem["SY"], C.get("SY"))
                for dc in range(8):
                    mg = t.matmul(PS[5][:, :],
                                  sgT[:, dc * 1024 + oc * 128: dc * 1024 + oc * 128 + 128],
                                  xn1t[:, dc * TOK1 + ck * 512: dc * TOK1 + (ck + 1) * 512],
                                  start=(dc == 0), stop=(dc == 7))
                inc(mg, "MM")
                for dc in range(8):
                    mu = t.matmul(PS[6][:, :],
                                  suT[:, dc * 1024 + oc * 128: dc * 1024 + oc * 128 + 128],
                                  xn1t[:, dc * TOK1 + ck * 512: dc * TOK1 + (ck + 1) * 512],
                                  start=(dc == 0), stop=(dc == 7))
                inc(mu, "MM")

            @block.scalar
            def _(s):
                s.wait_ge(sem["MM"], C.get("MM") - 1)
                s.wait_ge(sem["VE"], C.get("VE"))
                inc(s.activation(sil6[:, :], PS[5][:, :], AF.Silu), "SE")

            @block.vector
            def _(v):
                v.wait_ge(sem["MM"], C.get("MM"))
                v.wait_ge(sem["SE"], C.get("SE"))
                inc(v.scalar_tensor_tensor(hh6[:, oc * 512:(oc + 1) * 512],
                                           PS[6][:, :], 1.0, sil6[:, :],
                                           op0=ALU.mult, op1=ALU.mult), "VE")

        for ts_ in range(4):
            @block.tensor
            def _(t):
                t.wait_ge(sem["VE"], C.get("VE"))
                t.wait_ge(sem["GP"], C.get("GP"))
                for dh in range(2):
                    for oc in range(8):
                        mm = t.matmul(PS[5 + dh][:, :],
                                      hh6[:, oc * 512 + ts_ * 128: oc * 512 + ts_ * 128 + 128],
                                      sdT[:, oc * 1024 + dh * 512: oc * 1024 + (dh + 1) * 512],
                                      start=(oc == 0), stop=(oc == 7))
                    inc(mm, "MM")

            @block.vector
            def _(v):
                v.wait_ge(sem["MM"], C.get("MM"))
                for dh in range(2):
                    last = v.scalar_tensor_tensor(
                        prow6[:, dh * 512:(dh + 1) * 512], PS[5 + dh][:, :], 1.0,
                        a1ld[:, ts_ * 1024 + dh * 512: ts_ * 1024 + (dh + 1) * 512],
                        op0=ALU.mult, op1=ALU.add)
                inc(last, "VE")

            @block.gpsimd
            def _(g):
                g.wait_ge(sem["VE"], C.get("VE"))
                r0 = ck * 512 + ts_ * 128
                inc(g.dma_start(out=part1.ap()[r0:r0 + 128, :], in_=prow6[:, :]), "GP", 16)

    hard_barrier()
    p6b.close()

    p7 = ExitStack()
    w1T = p7.enter_context(nc.sbuf_tensor("w1T", [128, 8 * 512], BF16))
    w3T = p7.enter_context(nc.sbuf_tensor("w3T", [128, 8 * 512], BF16))
    w2T = p7.enter_context(nc.sbuf_tensor("w2T", [128, 4 * 1024], BF16))
    xe1 = p7.enter_context(nc.sbuf_tensor("xe1", [128, D + E], BF16))
    xe2 = p7.enter_context(nc.sbuf_tensor("xe2", [128, D + E], BF16))
    xeT = p7.enter_context(nc.sbuf_tensor("xeT", [128, 8 * PAD], BF16))
    hh7 = p7.enter_context(nc.sbuf_tensor("hh7", [128, 4 * PAD], BF16))
    sil7 = p7.enter_context(nc.sbuf_tensor("sil7", [128, PAD], BF16))
    ye7 = p7.enter_context(nc.sbuf_tensor("ye7", [128, 1024], BF16))
    wc1 = p7.enter_context(nc.sbuf_tensor("wc1", [128, 2], F32))

    bcreg = [None]

    @block.gpsimd
    def _(g):
        bcreg[0] = g.alloc_register("bcreg")
        g.reg_mov(bcreg[0], TOK1)

    for e in range(E):
        @block.sync
        def _(sy):
            sy.wait_ge(sem["MM"], C.get("MM"))
            if e == 0:
                sy.wait_ge(sem["CC"], C.get("CC"))
            for dc in range(8):
                inc(sy.dma_start(out=w1T[:, dc * 512:(dc + 1) * 512],
                                 in_=moe1.ap()[e][:, dc * 128:(dc + 1) * 128],
                                 transpose=True), "SY", 16)
                inc(sy.dma_start(out=w3T[:, dc * 512:(dc + 1) * 512],
                                 in_=moe3.ap()[e][:, dc * 128:(dc + 1) * 128],
                                 transpose=True), "SY", 16)
            for fc in range(4):
                inc(sy.dma_start(out=w2T[:, fc * 1024:(fc + 1) * 1024],
                                 in_=moe2.ap()[e][:, fc * 128:(fc + 1) * 128],
                                 transpose=True), "SY", 16)

        @block.gpsimd
        def _(g):
            g.wait_ge(sem["MM"], C.get("MM"))
            g.wait_ge(sem["VE"], C.get("VE"))
            inc(g.indirect_dma_start(
                xe1[:, :], None, xn1tab_h.ap(),
                bass.IndirectOffsetOnAxis(ap=gidxT0[:, e:e + 1], axis=0),
                bounds_check=bcreg[0], oob_is_err=False), "GP", 16)
            inc(g.indirect_dma_start(
                xe2[:, :], None, xn1tab_h.ap(),
                bass.IndirectOffsetOnAxis(ap=gidxT1[:, e:e + 1], axis=0),
                bounds_check=bcreg[0], oob_is_err=False), "GP", 16)

        @block.tensor
        def _(t):
            t.wait_ge(sem["GP"], C.get("GP"))
            for dc in range(8):
                mm = t.transpose(PS[7][:, :].bitcast(BF16)[:, dc * 128:(dc + 1) * 128],
                                 xe1[:, dc * 128:(dc + 1) * 128], ID)
            inc(mm, "MM")
            for dc in range(8):
                mm = t.transpose(PS[0][:, :].bitcast(BF16)[:, dc * 64:(dc + 1) * 64],
                                 xe2[0:64, dc * 128:(dc + 1) * 128], ID[0:64, 0:64])
            inc(mm, "MM")

        @block.vector
        def _(v):
            v.wait_ge(sem["MM"], C.get("MM"))
            inc(v.tensor_copy(
                xeT[:, :].rearrange("p (a s) -> p a s", a=8)[:, :, 0:128],
                PS[7][:, :].bitcast(BF16).rearrange("p (a s) -> p a s", a=8)), "VE")
            inc(v.tensor_copy(
                xeT[:, :].rearrange("p (a s) -> p a s", a=8)[:, :, 128:192],
                PS[0][:, :].bitcast(BF16)[:, 0:512].rearrange("p (a s) -> p a s", a=8)),
                "VE")
            inc(v.tensor_copy(wc1[:, 0:1], xe1[:, D + e:D + e + 1]), "VE")
            inc(v.tensor_copy(wc1[0:64, 1:2], xe2[0:64, D + e:D + e + 1]), "VE")
            vw(v)

        for fc in range(4):
            @block.tensor
            def _(t):
                t.wait_ge(sem["VE"], C.get("VE"))
                t.wait_ge(sem["SE"], C.get("SE"))
                t.wait_ge(sem["SY"], C.get("SY"))
                for dc in range(8):
                    m1 = t.matmul(PS[1][:, 0:PAD],
                                  w1T[:, dc * 512 + fc * 128: dc * 512 + fc * 128 + 128],
                                  xeT[:, dc * PAD:(dc + 1) * PAD],
                                  start=(dc == 0), stop=(dc == 7))
                inc(m1, "MM")
                for dc in range(8):
                    m3 = t.matmul(PS[2][:, 0:PAD],
                                  w3T[:, dc * 512 + fc * 128: dc * 512 + fc * 128 + 128],
                                  xeT[:, dc * PAD:(dc + 1) * PAD],
                                  start=(dc == 0), stop=(dc == 7))
                inc(m3, "MM")

            @block.scalar
            def _(s):
                s.wait_ge(sem["MM"], C.get("MM") - 1)
                s.wait_ge(sem["VE"], C.get("VE"))
                inc(s.activation(sil7[:, :], PS[1][:, 0:PAD], AF.Silu), "SE")

            @block.vector
            def _(v):
                v.wait_ge(sem["MM"], C.get("MM"))
                v.wait_ge(sem["SE"], C.get("SE"))
                inc(v.scalar_tensor_tensor(hh7[:, fc * PAD:(fc + 1) * PAD],
                                           PS[2][:, 0:PAD], 1.0, sil7[:, :],
                                           op0=ALU.mult, op1=ALU.mult), "VE")

        for st_ in range(2):
            rows = 128 if st_ == 0 else 64

            @block.tensor
            def _(t):
                t.wait_ge(sem["VE"], C.get("VE"))
                t.wait_ge(sem["GP"], C.get("GP"))
                for dh in range(2):
                    for fc in range(4):
                        mm = t.matmul(PS[3 + dh][0:rows, :],
                                      hh7[:, fc * PAD + st_ * 128: fc * PAD + st_ * 128 + rows],
                                      w2T[:, fc * 1024 + dh * 512: fc * 1024 + (dh + 1) * 512],
                                      start=(fc == 0), stop=(fc == 3))
                    inc(mm, "MM")

            @block.vector
            def _(v):
                v.wait_ge(sem["MM"], C.get("MM"))
                for dh in range(2):
                    last = v.tensor_scalar(ye7[0:rows, dh * 512:(dh + 1) * 512],
                                           PS[3 + dh][0:rows, :], wc1[0:rows, st_:st_ + 1],
                                           None, op0=ALU.mult)
                inc(last, "VE")

            @block.gpsimd
            def _(g):
                g.wait_ge(sem["VE"], C.get("VE"))
                r0 = 1 + e * PAD + st_ * 128
                inc(g.dma_start(out=yetab.ap()[r0:r0 + rows, :], in_=ye7[0:rows, :]),
                    "GP", 16)

    hard_barrier()
    p7.close()

    p8 = ExitStack()
    acc8 = p8.enter_context(nc.sbuf_tensor("acc8", [128, 8 * 1024], BF16))
    x1r8 = p8.enter_context(nc.sbuf_tensor("x1r8", [128, 1024], F32))
    p1r8 = p8.enter_context(nc.sbuf_tensor("p1r8", [128, 1024], BF16))
    yt8 = p8.enter_context(nc.sbuf_tensor("yt8", [128, 1024], F32))

    @block.gpsimd
    def _(g):
        g.reg_mov(bcreg[0], E * PAD)

    for tt in range(32):
        @block.gpsimd
        def _(g):
            g.wait_ge(sem["VE"], C.get("VE"))
            for r in range(8):
                inc(g.indirect_dma_start(
                    acc8[:, r * 1024:(r + 1) * 1024], None, yetab.ap(),
                    bass.IndirectOffsetOnAxis(ap=destbuf[:, tt * 8 + r: tt * 8 + r + 1],
                                              axis=0),
                    bounds_check=bcreg[0], oob_is_err=False), "GP", 16)

        @block.sync
        def _(sy):
            sy.wait_ge(sem["VE"], C.get("VE"))
            for ss in range(8):
                inc(sy.dma_start(out=x1r8[ss * 16:(ss + 1) * 16, :],
                                 in_=x1v[:, 8 * tt + ss, :]), "SY", 16)
            inc(sy.dma_start(out=p1r8[:, :], in_=part1.ap()[tt * 128:(tt + 1) * 128, :]),
                "SY", 16)

        @block.vector
        def _(v):
            v.wait_ge(sem["GP"], C.get("GP"))
            v.wait_ge(sem["SY"], C.get("SY"))
            v.tensor_tensor(yt8[:, :], x1r8[:, :], p1r8[:, :], op=ALU.add)
            for r in range(8):
                last = v.tensor_tensor(yt8[:, :], yt8[:, :],
                                       acc8[:, r * 1024:(r + 1) * 1024], op=ALU.add)
            inc(last, "VE")

        @block.gpsimd
        def _(g):
            g.wait_ge(sem["VE"], C.get("VE"))
            for ss in range(8):
                inc(g.dma_start(out=y_out[:, 8 * tt + ss, :],
                                in_=yt8[ss * 16:(ss + 1) * 16, :]), "GP", 16)

    if l1_only:
        @block.gpsimd
        def _(g):
            g.wait_ge(sem["GP"], C.get("GP"))
            inc(g.dma_start(out=env["dbg_xn1"], in_=xn1tab_h.ap()), "GP", 16)
            inc(g.dma_start(out=env["dbg_part1"], in_=part1.ap()), "GP", 16)
            inc(g.dma_start(out=env["dbg_ye"], in_=yetab.ap()), "GP", 16)
            inc(g.dma_start(out=env["dbg_dest"], in_=destbuf[:, :]), "GP", 16)
            inc(g.dma_start(out=env["dbg_gidx"][:, 0:64], in_=gidxT0[:, :]), "GP", 16)
            inc(g.dma_start(out=env["dbg_gidx"][:, 64:128], in_=gidxT1[:, :]), "GP", 16)

    hard_barrier()
    p8.close()
    l1c.close()

    @block.gpsimd
    def _(g):
        g.wait_ge(sem["GP"], C.get("GP"))
        g.wait_ge(sem["SY"], C.get("SY"))
        g.wait_ge(sem["VE"], C.get("VE"))
        g.wait_ge(sem["MM"], C.get("MM"))
        g.wait_ge(sem["SE"], C.get("SE"))



def make_consts(core_id):
    ID = np.eye(128)
    triu = np.triu(np.full((128, 128), NEG, np.float32), 1)
    tril = np.tril(np.full((128, 128), NEG, np.float32), -1)
    sv = np.arange(128) // 16
    bv = np.arange(128) % 16
    u_tile = ((bv[:, None] == bv[None, :]) & (sv[:, None] < sv[None, :])).astype(np.float32)
    u_plain = (np.arange(128)[:, None] < np.arange(128)[None, :]).astype(np.float32)
    eb = (bv[:, None] == np.arange(16)[None, :]).astype(np.float32)
    ebt = np.ascontiguousarray(eb.T)
    jv = np.arange(128) // 16
    bpv = np.arange(128) % 16
    u1 = ((bpv[:, None] < np.arange(16)[None, :]) |
          ((bpv[:, None] == np.arange(16)[None, :]) & (jv[:, None] < core_id))).astype(np.float32)
    bd = (np.arange(128)[:, None] // 16 == (np.arange(512)[None, :] % 128) // 16)
    return {
        "id_bf": ID.astype(ml_dtypes.bfloat16),
        "id_f32": ID.astype(np.float32),
        "triu_bf": triu.astype(ml_dtypes.bfloat16),
        "tril_bf": tril.astype(ml_dtypes.bfloat16),
        "u_tile": u_tile,
        "u_plain": u_plain,
        "eb_mat": eb,
        "ebt_mat": ebt,
        "u1_mine": u1,
        "e192_row": (np.arange(E) * PAD + 1).astype(np.float32)[None, :],
        "bd16_mask": bd.astype(ml_dtypes.bfloat16),
        "iota_col": np.arange(128, dtype=np.float32)[:, None],
    }


def make_in_maps(inputs, l0_only=False, l1_only=False, x1=None):
    maps = []
    for c in range(NCORES):
        cst = make_consts(c)
        m = {}
        for k in ["id_bf", "id_f32", "triu_bf", "tril_bf"]:
            m[k] = cst[k]
        m["x_slice"] = np.ascontiguousarray(inputs["L0_attn_x"][2 * c:2 * c + 2])
        for k in ["L0_attn_P_q", "L0_attn_P_k", "L0_attn_P_v", "L0_attn_P_o",
                  "L0_ffn_w_gate", "L0_ffn_w_up", "L0_ffn_w_down"]:
            m[k] = np.ascontiguousarray(inputs[k])
        if not l0_only:
            for k in ["u_tile", "u_plain", "eb_mat", "ebt_mat", "u1_mine",
                      "e192_row", "bd16_mask", "iota_col"]:
                m[k] = cst[k]
            m["L1_attn_Wqkv"] = np.ascontiguousarray(inputs["L1_attn_Wqkv"])
            m["L1_attn_bqkv"] = np.ascontiguousarray(inputs["L1_attn_bqkv"]).reshape(1, -1)
            m["L1_attn_seq_gamma"] = np.ascontiguousarray(inputs["L1_attn_seq_gamma"]).reshape(1, -1)
            m["L1_attn_seq_beta"] = np.ascontiguousarray(inputs["L1_attn_seq_beta"]).reshape(1, -1)
            m["L1_ffn_gate_weight"] = np.ascontiguousarray(inputs["L1_ffn_gate_weight"])
            m["L1_ffn_e_bias"] = np.ascontiguousarray(inputs["L1_ffn_e_bias"]).reshape(1, -1)
            ne = E // NCORES
            m["w1_my"] = np.ascontiguousarray(inputs["L1_ffn_w1"][c * ne:(c + 1) * ne])
            m["w2_my"] = np.ascontiguousarray(inputs["L1_ffn_w2"][c * ne:(c + 1) * ne])
            m["w3_my"] = np.ascontiguousarray(inputs["L1_ffn_w3"][c * ne:(c + 1) * ne])
            m["L1_ffn_shared_gate"] = np.ascontiguousarray(inputs["L1_ffn_shared_gate"])
            m["L1_ffn_shared_up"] = np.ascontiguousarray(inputs["L1_ffn_shared_up"])
            m["L1_ffn_shared_down"] = np.ascontiguousarray(inputs["L1_ffn_shared_down"])
            if l1_only:
                m["x1_slice"] = np.ascontiguousarray(
                    x1[:, c * SPC:(c + 1) * SPC, :].reshape(TOK1, D))
        maps.append(m)
    return maps


def kernel(**inputs):
    inputs = {k: np.asarray(v) for k, v in inputs.items()}
    nc = build_program()
    in_maps = make_in_maps(inputs)
    res = run_bass_kernel_spmd(nc, in_maps, core_ids=list(range(NCORES)))
    out = np.empty((B, S, D), np.float32)
    for c in range(NCORES):
        out[:, c * SPC:(c + 1) * SPC, :] = res.results[c]["y_slice"]
    return out


# version 04 (final)
# speedup vs baseline: 1.1644x; 1.1644x over previous
import numpy as np
import ml_dtypes
import concourse.bass as bass
import concourse.mybir as mybir
from concourse.bass_utils import run_bass_kernel_spmd
from concourse.aot_env import allow_ici_all_to_all
from contextlib import ExitStack

F32 = mybir.dt.float32
BF16 = mybir.dt.bfloat16
I32 = mybir.dt.int32
AF = mybir.ActivationFunctionType
ALU = mybir.AluOpType
AX = mybir.AxisListType.X

NCORES = 8
B, S, D = 16, 2048, 1024
BPC = 2
SPC = S // NCORES
TOK0 = BPC * S
TOK1 = B * SPC
NH = 16
DK = 64
FH = 3072
E, TOPK, NG = 64, 8, 8
CAP = 1024
PAD = 192
EBIG = 20000.0
NEG = -1e30



_SCOPE_IDS = {}


def _scope_enter(nc, name):
    _SCOPE_IDS[name] = nc.enter_named_scope(name, False)[0]


def _scope_leave(nc, name):
    nc.leave_named_scope(name, _SCOPE_IDS[name], False)

class Counts:
    def __init__(self):
        self.v = {}

    def inc(self, name, n):
        self.v[name] = self.v.get(name, 0) + n
        return self.v[name]

    def get(self, name):
        return self.v.get(name, 0)



def build_program(l0_only=False, l1_only=False):
    nc = bass.Bass("TRN2", target_bir_lowering=False, num_devices=NCORES)

    x_in = nc.dram_tensor("x_slice", [BPC, S, D], F32, kind="ExternalInput").ap()
    pq_in = nc.dram_tensor("L0_attn_P_q", [NH, D, DK], F32, kind="ExternalInput").ap()
    pk_in = nc.dram_tensor("L0_attn_P_k", [NH, D, DK], F32, kind="ExternalInput").ap()
    pv_in = nc.dram_tensor("L0_attn_P_v", [NH, D, DK], F32, kind="ExternalInput").ap()
    po_in = nc.dram_tensor("L0_attn_P_o", [NH, D, DK], F32, kind="ExternalInput").ap()
    wg_in = nc.dram_tensor("L0_ffn_w_gate", [FH, D], I32, kind="ExternalInput").ap()
    wu_in = nc.dram_tensor("L0_ffn_w_up", [FH, D], I32, kind="ExternalInput").ap()
    wd_in = nc.dram_tensor("L0_ffn_w_down", [D, FH], I32, kind="ExternalInput").ap()
    id_bf = nc.dram_tensor("id_bf", [128, 128], BF16, kind="ExternalInput").ap()
    id_f32 = nc.dram_tensor("id_f32", [128, 128], F32, kind="ExternalInput").ap()
    triu_bf = nc.dram_tensor("triu_bf", [128, 128], BF16, kind="ExternalInput").ap()
    tril_bf = nc.dram_tensor("tril_bf", [128, 128], BF16, kind="ExternalInput").ap()
    if not l0_only:
        wqkv_in = nc.dram_tensor("L1_attn_Wqkv", [3 * D, D], F32, kind="ExternalInput").ap()
        bqkv_in = nc.dram_tensor("L1_attn_bqkv", [1, 3 * D], F32, kind="ExternalInput").ap()
        gamma_in = nc.dram_tensor("L1_attn_seq_gamma", [1, NH], F32, kind="ExternalInput").ap()
        beta_in = nc.dram_tensor("L1_attn_seq_beta", [1, NH], F32, kind="ExternalInput").ap()
        gw_in = nc.dram_tensor("L1_ffn_gate_weight", [E, D], F32, kind="ExternalInput").ap()
        ebias_in = nc.dram_tensor("L1_ffn_e_bias", [1, E], F32, kind="ExternalInput").ap()
        w1_my = nc.dram_tensor("w1_my", [E // NCORES, 512, D], F32, kind="ExternalInput").ap()
        w2_my = nc.dram_tensor("w2_my", [E // NCORES, D, 512], F32, kind="ExternalInput").ap()
        w3_my = nc.dram_tensor("w3_my", [E // NCORES, 512, D], F32, kind="ExternalInput").ap()
        sg_in = nc.dram_tensor("L1_ffn_shared_gate", [D, D], F32, kind="ExternalInput").ap()
        su_in = nc.dram_tensor("L1_ffn_shared_up", [D, D], F32, kind="ExternalInput").ap()
        sd_in = nc.dram_tensor("L1_ffn_shared_down", [D, D], F32, kind="ExternalInput").ap()
        u_tile = nc.dram_tensor("u_tile", [128, 128], F32, kind="ExternalInput").ap()
        u_plain = nc.dram_tensor("u_plain", [128, 128], F32, kind="ExternalInput").ap()
        eb_mat = nc.dram_tensor("eb_mat", [128, 16], F32, kind="ExternalInput").ap()
        ebt_mat = nc.dram_tensor("ebt_mat", [16, 128], F32, kind="ExternalInput").ap()
        u1_mine = nc.dram_tensor("u1_mine", [128, 16], F32, kind="ExternalInput").ap()
        e192_row = nc.dram_tensor("e192_row", [1, E], F32, kind="ExternalInput").ap()
        bd16_mask = nc.dram_tensor("bd16_mask", [128, 512], BF16, kind="ExternalInput").ap()
        iota_col = nc.dram_tensor("iota_col", [128, 1], F32, kind="ExternalInput").ap()
        y_out = nc.dram_tensor("y_slice", [B, SPC, D], F32, kind="ExternalOutput").ap()
        wqkv_b = nc.dram_tensor("wqkv_b", [3 * D, D], BF16)
        sg_b = nc.dram_tensor("sg_b", [D, D], BF16)
        su_b = nc.dram_tensor("su_b", [D, D], BF16)
        sd_b = nc.dram_tensor("sd_b", [D, D], BF16)
        gw_bd = nc.dram_tensor("gw_bd", [E, D], BF16)
        moe1 = nc.dram_tensor("moe1", [E, 512, D], BF16)
        moe3 = nc.dram_tensor("moe3", [E, 512, D], BF16)
        moe2 = nc.dram_tensor("moe2", [E, D, 512], BF16)
        moe1_my = nc.dram_tensor("moe1_my", [E // NCORES, 512, D], BF16)
        moe3_my = nc.dram_tensor("moe3_my", [E // NCORES, 512, D], BF16)
        moe2_my = nc.dram_tensor("moe2_my", [E // NCORES, D, 512], BF16)
        xn1tab = nc.dram_tensor("xn1tab", [TOK1 + 1, D + E], BF16)
        yetab = nc.dram_tensor("yetab", [E * PAD + 1, D], BF16)
        part1 = nc.dram_tensor("part1", [TOK1, D], BF16)
        cnt_bounce = nc.dram_tensor("cnt_bounce", [16, E], F32)
        cnt_all = nc.dram_tensor("cnt_all", [NCORES * 16, E], F32)

    if l0_only:
        x1_out = nc.dram_tensor("x1_dbg", [TOK0, D], F32, kind="ExternalOutput").ap()
        attn_dbg = nc.dram_tensor("attn_dbg", [TOK0, D], BF16, kind="ExternalOutput").ap()
        ffnp_dbg = nc.dram_tensor("ffnp_dbg", [3, TOK0, D], BF16, kind="ExternalOutput").ap()
        xt0_dbg = nc.dram_tensor("xt0_dbg", [BPC, 8, 128, S], BF16, kind="ExternalOutput").ap()
        negm_dbg = nc.dram_tensor("negm_dbg", [65, 2 * S], BF16, kind="ExternalOutput").ap()
        nmc_dbg = nc.dram_tensor("nmc_dbg", [128, 64], F32, kind="ExternalOutput").ap()

    po_bf = nc.dram_tensor("po_bf", [D, NH * DK], BF16)
    wg_b = nc.dram_tensor("wg_b", [FH, D], BF16)
    wu_b = nc.dram_tensor("wu_b", [FH, D], BF16)
    wd_b = nc.dram_tensor("wd_b", [D, FH], BF16)
    xt0_dram = nc.dram_tensor("xt0_dram", [BPC, 8, 128, S], BF16)
    attn_rows = nc.dram_tensor("attn_rows", [TOK0, D], BF16)
    ffnp = nc.dram_tensor("ffnp", [3, TOK0, D], BF16)
    negm_stage = nc.dram_tensor("negm_stage", [16, 128], BF16)
    a2a_in = nc.dram_tensor("a2a_in", [TOK0, D], F32)
    if not l0_only:
        if l1_only:
            a2a_out = nc.dram_tensor("x1_slice", [TOK1, D], F32, kind="ExternalInput").ap()
            dbg_xn1 = nc.dram_tensor("dbg_xn1", [TOK1 + 1, D + E], BF16, kind="ExternalOutput").ap()
            dbg_part1 = nc.dram_tensor("dbg_part1", [TOK1, D], BF16, kind="ExternalOutput").ap()
            dbg_ye = nc.dram_tensor("dbg_ye", [E * PAD + 1, D], BF16, kind="ExternalOutput").ap()
            dbg_af = nc.dram_tensor("dbg_af", [128, 2048], F32, kind="ExternalOutput").ap()
            dbg_dest = nc.dram_tensor("dbg_dest", [128, 256], I32, kind="ExternalOutput").ap()
            dbg_gidx = nc.dram_tensor("dbg_gidx", [128, 128], I32, kind="ExternalOutput").ap()
            dbg_cnt = nc.dram_tensor("dbg_cnt", [128, 192], F32, kind="ExternalOutput").ap()
        else:
            a2a_out_t = nc.dram_tensor("a2a_out", [TOK1, D], F32)
            a2a_out = a2a_out_t.ap()

    C = Counts()

    with ExitStack() as top:
        sem = {}
        for name in ["SY", "GP", "MM", "VE", "SE", "CC", "ST"]:
            sem[name] = top.enter_context(nc.semaphore("sem" + name))

        def inc(inst, name, n=1):
            inst.then_inc(sem[name], n)
            return C.inc(name, n)

        def hard_barrier0():
            for engname in ("tensor", "vector", "scalar", "sync", "gpsimd"):
                def _bar(eng):
                    for s_ in ("SY", "GP", "MM", "VE", "SE", "CC", "ST"):
                        eng.wait_ge(sem[s_], C.get(s_))
                getattr(block, engname)(_bar)

        PS = [top.enter_context(nc.psum_tensor(f"ps{i}", [128, 512], F32))
              for i in range(8)]

        def ps_bf(i):
            return PS[i][:, :].bitcast(BF16)

        block = top.enter_context(nc.Block())

        consts = top.enter_context(nc.sbuf_tensor("consts", [128, 128 * 3], BF16))
        idf_sb = top.enter_context(nc.sbuf_tensor("idf_sb", [128, 128], F32))
        ones_bf = top.enter_context(nc.sbuf_tensor("ones_bf", [128, 128], BF16))

        @block.sync
        def _(sy):
            inc(sy.dma_start(out=consts[:, 0:128], in_=id_bf), "SY", 16)
            inc(sy.dma_start(out=consts[:, 128:256], in_=triu_bf), "SY", 16)
            inc(sy.dma_start(out=consts[:, 256:384], in_=tril_bf), "SY", 16)
            inc(sy.dma_start(out=idf_sb[:, :], in_=id_f32), "SY", 16)

        epscol = top.enter_context(nc.sbuf_tensor("epscol", [128, 1], F32))

        @block.vector
        def _(v):
            inc(v.memset(ones_bf[:, :], 1.0), "VE")
            inc(v.memset(epscol[:, :], 1e-5), "VE")

        ID = consts[:, 0:128]
        TRIU = consts[:, 128:256]
        TRIL = consts[:, 256:384]

        if not l1_only:
            _scope_enter(nc, "l0_prep")
            p01 = ExitStack()
            pqs = p01.enter_context(nc.sbuf_tensor("pqs", [128, 8192], BF16))
            pks = p01.enter_context(nc.sbuf_tensor("pks", [128, 8192], BF16))
            pvs = p01.enter_context(nc.sbuf_tensor("pvs", [128, 8192], BF16))
            pots = p01.enter_context(nc.sbuf_tensor("pots", [128, 8 * 1024], BF16))

            @block.gpsimd
            def _(g):
                for tgt, src in ((pqs, pq_in), (pks, pk_in), (pvs, pv_in)):
                    for h in range(NH):
                        inc(g.dma_start(
                            out=tgt[:, :].rearrange("p (a h v) -> p a h v", a=8, h=16)[:, :, h, :],
                            in_=src[h].rearrange("(a p) v -> p a v", p=128)), "GP", 16)
                for h in range(NH):
                    inc(g.dma_start(out=po_bf[:, h * DK:(h + 1) * DK]
                                    .rearrange("(a p) v -> p a v", p=128),
                                    in_=po_in[h].rearrange("(a p) v -> p a v", p=128)), "GP", 16)

            @block.vector
            def _(v):
                v.wait_ge(sem["GP"], C.get("GP"))
                inc(v.tensor_scalar_mul(pqs[:, :], pqs[:, :], 8.0), "VE")

            with ExitStack() as pp:
                istage = pp.enter_context(nc.sbuf_tensor("istage", [128, FH], F32))
                bstage = pp.enter_context(nc.sbuf_tensor("bstage", [128, FH], BF16))
                jobs = []
                for (dst, src, rows, cols) in ((wg_b, wg_in, FH, D), (wu_b, wu_in, FH, D),
                                               (wd_b, wd_in, D, FH)):
                    for r0 in range(0, rows, 128):
                        jobs.append((dst, src, r0, cols))
                for (dst, src, r0, cols) in jobs:
                    @block.gpsimd
                    def _(g):
                        g.wait_ge(sem["VE"], C.get("VE"))
                        inc(g.dma_start(out=istage[:, :cols], in_=src[r0:r0 + 128, :]), "GP", 16)

                    @block.vector
                    def _(v):
                        v.wait_ge(sem["GP"], C.get("GP"))
                        v.wait_ge(sem["SY"], C.get("SY"))
                        inc(v.tensor_copy(bstage[:, :cols], istage[:, :cols]), "VE")

                    @block.sync
                    def _(sy):
                        sy.wait_ge(sem["VE"], C.get("VE"))
                        inc(sy.dma_start(out=dst[r0:r0 + 128, :], in_=bstage[:, :cols]), "SY", 16)

                @block.gpsimd
                def _(g):
                    g.wait_ge(sem["SY"], C.get("SY"))

            @block.sync
            def _(sy):
                sy.wait_ge(sem["GP"], C.get("GP"))
                for hp in range(8):
                    inc(sy.dma_start(out=pots[:, hp * 1024:(hp + 1) * 1024],
                                     in_=po_bf[:, hp * 128:(hp + 1) * 128], transpose=True),
                        "SY", 16)

            if not l0_only:
                emit_l1_casts(nc, block, sem, C, inc, locals())

            xt0 = p01.enter_context(nc.sbuf_tensor("xt0", [128, 8 * S], BF16))
            lnrow2 = p01.enter_context(nc.sbuf_tensor("lnrow2", [128, 1024], F32))
            lnsc2 = p01.enter_context(nc.sbuf_tensor("lnsc2", [128, 8], F32))
            xnrow2 = p01.enter_context(nc.sbuf_tensor("xnrow2", [128, 1024], BF16))
            vsb = p01.enter_context(nc.sbuf_tensor("vsb", [128, 16 * (NH * 65)], BF16))
            onorm = p01.enter_context(nc.sbuf_tensor("onorm", [128, 8 * S], BF16))
            qt8 = p01.enter_context(nc.sbuf_tensor("qt8", [128, S], BF16))
            kt8 = p01.enter_context(nc.sbuf_tensor("kt8", [128, S], BF16))
            expt = p01.enter_context(nc.sbuf_tensor("expt", [128, 2048], BF16))
            negm = p01.enter_context(nc.sbuf_tensor("negm", [65, 2 * S], BF16))
            nmcols = p01.enter_context(nc.sbuf_tensor("nmcols", [128, 64], F32))
            lnrow = p01.enter_context(nc.sbuf_tensor("lnrow", [128, 1024], F32))
            lnsc = p01.enter_context(nc.sbuf_tensor("lnsc", [128, 8], F32))
            xnrow = p01.enter_context(nc.sbuf_tensor("xnrow", [128, 1024], BF16))
            otmp = p01.enter_context(nc.sbuf_tensor("otmp", [64, 512], BF16))
            rrow = p01.enter_context(nc.sbuf_tensor("rrow", [1, 512], BF16))

            _scope_leave(nc, "l0_prep")
            _scope_enter(nc, "l0_main")
            se0 = {}
            for b in range(BPC):
                marks = {}
                for tt in range(16):
                    LR = lnrow if tt % 2 == 0 else lnrow2
                    LS = lnsc if tt % 2 == 0 else lnsc2
                    XR = xnrow if tt % 2 == 0 else xnrow2
                    PB = 6 + tt % 2

                    @block.sync
                    def _(sy):
                        if tt >= 2:
                            sy.wait_ge(sem["VE"], marks[tt - 2])
                        else:
                            sy.wait_ge(sem["MM"], C.get("MM"))
                        if (b, tt) > (0, 0):
                            sy.wait_ge(sem["SE"], se0[(b, tt - 1) if tt else (b - 1, 15)])
                        inc(sy.dma_start(out=LR[:, :],
                                         in_=x_in[b, tt * 128:(tt + 1) * 128, :]), "SY", 16)

                    @block.scalar
                    def _(s):
                        s.wait_ge(sem["SY"], C.get("SY"))
                        s.activation(XR[:, :], LR[:, :], AF.Copy, scale=1.0 / D,
                                     accum_out=LS[:, 1:2])
                        se0[(b, tt)] = inc(s.activation(XR[:, :], LR[:, :], AF.Square,
                                         scale=1.0 / float(np.sqrt(D)),
                                         accum_out=LS[:, 2:3]), "SE")

                    @block.vector
                    def _(v):
                        v.wait_ge(sem["SE"], C.get("SE"))
                        inc(v.tensor_tensor(LS[:, 3:4], LS[:, 1:2], LS[:, 1:2], op=ALU.mult), "VE")
                        v.wait_ge(sem["VE"], C.get("VE"))
                        inc(v.tensor_tensor(LS[:, 5:6], LS[:, 2:3], LS[:, 3:4],
                                            op=ALU.subtract), "VE")

                    @block.scalar
                    def _(s):
                        s.wait_ge(sem["VE"], C.get("VE"))
                        inc(s.activation(LS[:, 7:8], LS[:, 5:6], AF.Ln, bias=epscol[:, 0:1]), "SE")
                        s.wait_ge(sem["SE"], C.get("SE"))
                        inc(s.activation(LS[:, 0:1], LS[:, 7:8], AF.Exp, scale=-0.5), "SE")

                    @block.vector
                    def _(v):
                        v.wait_ge(sem["SE"], C.get("SE"))
                        inc(v.tensor_scalar(XR[:, :], LR[:, :], LS[:, 1:2], LS[:, 0:1],
                                            op0=ALU.subtract, op1=ALU.mult), "VE")

                    @block.tensor
                    def _(t):
                        t.wait_ge(sem["VE"], C.get("VE"))
                        for dc in range(8):
                            last = t.transpose(ps_bf(PB)[:, dc * 128:(dc + 1) * 128],
                                               XR[:, dc * 128:(dc + 1) * 128], ID)
                        inc(last, "MM")

                    @block.vector
                    def _(v):
                        v.wait_ge(sem["MM"], C.get("MM"))
                        marks[tt] = inc(v.tensor_copy(
                            xt0[:, :].rearrange("p (a s) -> p a s", a=8)[:, :, tt * 128:(tt + 1) * 128],
                            ps_bf(PB)[:, :].rearrange("p (a s) -> p a s", a=8)), "VE")

                @block.sync
                def _(sy):
                    sy.wait_ge(sem["VE"], C.get("VE"))
                    for dc in range(8):
                        inc(sy.dma_start(out=xt0_dram[b, dc], in_=xt0[:, dc * S:(dc + 1) * S]),
                            "SY", 16)

                for tt in range(16):
                    @block.tensor
                    def _(t):
                        for hg in range(2):
                            t.wait_ge(sem["VE"], C.get("VE"))
                            for dc in range(8):
                                last = t.matmul(PS[hg][:, :],
                                                xt0[:, dc * S + tt * 128: dc * S + (tt + 1) * 128],
                                                pvs[:, dc * 1024 + hg * 512: dc * 1024 + (hg + 1) * 512],
                                                start=(dc == 0), stop=(dc == 7))
                            inc(last, "MM")

                    @block.vector
                    def _(v):
                        base = tt * (NH * 65)
                        for hg in range(2):
                            v.wait_ge(sem["MM"], C.get("MM") - (1 - hg))
                            inc(v.tensor_copy(
                                vsb[:, base + hg * 8 * 65: base + (hg * 8 + 8) * 65]
                                .rearrange("p (h v) -> p h v", h=8)[:, :, 0:64],
                                PS[hg][:, :].rearrange("p (h v) -> p h v", h=8)), "VE")
                        inc(v.memset(vsb[:, base:base + NH * 65]
                                     .rearrange("p (h v) -> p h v", h=NH)[:, :, 64:65], 1.0), "VE")

                for hp in range(8):
                    for tc in range(4):
                        @block.tensor
                        def _(t):
                            t.wait_ge(sem["VE"], C.get("VE"))
                            for w_sb, bank in ((pqs, 2), (pks, 3)):
                                for dc in range(8):
                                    last = t.matmul(
                                        PS[bank][:, :],
                                        w_sb[:, dc * 1024 + hp * 128: dc * 1024 + hp * 128 + 128],
                                        xt0[:, dc * S + tc * 512: dc * S + (tc + 1) * 512],
                                        start=(dc == 0), stop=(dc == 7))
                                inc(last, "MM")

                        @block.vector
                        def _(v):
                            v.wait_ge(sem["MM"], C.get("MM"))
                            v.tensor_copy(qt8[:, tc * 512:(tc + 1) * 512], PS[2][:, :])
                            inc(v.tensor_copy(kt8[:, tc * 512:(tc + 1) * 512], PS[3][:, :]), "VE")

                    for qt in range(16):
                        nchunks = qt // 4 + 1

                        @block.tensor
                        def _(t):
                            t.wait_ge(sem["VE"], C.get("VE"))
                            for kc in range(nchunks):
                                n = 512 if kc < qt // 4 else (qt * 128 - kc * 512 + 128)
                                diag = (kc == qt // 4)
                                mmA = t.matmul(PS[kc][:, 0:n], qt8[0:64, qt * 128:(qt + 1) * 128],
                                               kt8[0:64, kc * 512: kc * 512 + n],
                                               start=True, stop=not diag, tile_position=(0, 0))
                                mmB = t.matmul(PS[4 + kc][:, 0:n], qt8[64:128, qt * 128:(qt + 1) * 128],
                                               kt8[64:128, kc * 512: kc * 512 + n],
                                               start=True, stop=not diag, tile_position=(64, 0))
                                if diag:
                                    off = qt * 128 - kc * 512
                                    mmA = t.matmul(PS[kc][:, off:off + 128], ID, TRIU,
                                                   start=False, stop=True)
                                    mmB = t.matmul(PS[4 + kc][:, off:off + 128], ID, TRIU,
                                                   start=False, stop=True)
                                inc(mmB, "MM")

                        @block.vector
                        def _(v):
                            for kc in range(nchunks):
                                n = 512 if kc < qt // 4 else (qt * 128 - kc * 512 + 128)
                                v.wait_ge(sem["MM"], C.get("MM") - (nchunks - 1 - kc))
                                v.reduce_max(lnsc[:, kc:kc + 1], PS[kc][:, 0:n], axis=AX)
                                inc(v.reduce_max(lnsc[:, 4 + kc:5 + kc], PS[4 + kc][:, 0:n],
                                                 axis=AX), "VE")
                            v.wait_ge(sem["VE"], C.get("VE"))
                            v.reduce_max(nmcols[:, qt:qt + 1], lnsc[:, 0:nchunks], axis=AX)
                            inc(v.reduce_max(nmcols[:, 16 + qt:17 + qt], lnsc[:, 4:4 + nchunks],
                                             axis=AX), "VE")

                    @block.vector
                    def _(v):
                        v.wait_ge(sem["VE"], C.get("VE"))
                        inc(v.tensor_scalar_mul(nmcols[:, 32:64], nmcols[:, 0:32], -1.0), "VE")
                        v.wait_ge(sem["VE"], C.get("VE"))

                    for hh in range(2):
                        @block.tensor
                        def _(t):
                            t.wait_ge(sem["VE"], C.get("VE"))
                            inc(t.transpose(PS[7][0:16, 0:128].bitcast(F32) if False else PS[7][0:16, 0:128],
                                            nmcols[:, 32 + hh * 16: 48 + hh * 16],
                                            idf_sb[:, :]), "MM")

                        @block.vector
                        def _(v):
                            v.wait_ge(sem["MM"], C.get("MM"))
                            v.wait_ge(sem["SY"], C.get("SY"))
                            inc(v.tensor_copy(xnrow[0:16, 0:128], PS[7][0:16, 0:128]), "VE")

                        @block.sync
                        def _(sy):
                            sy.wait_ge(sem["VE"], C.get("VE"))
                            inc(sy.dma_start(out=negm_stage.ap(), in_=xnrow[0:16, 0:128]), "SY", 16)
                            sy.wait_ge(sem["SY"], C.get("SY"))
                            inc(sy.dma_start(
                                out=negm[hh * 64: hh * 64 + 1, hh * S: hh * S + S],
                                in_=negm_stage.ap().rearrange("a b -> (a b)")[None, :]), "SY", 16)

                    if l0_only and b == 0 and hp == 0:
                        @block.sync
                        def _(sy):
                            sy.wait_ge(sem["SY"], C.get("SY"))
                            inc(sy.dma_start(out=negm_dbg, in_=negm[:, :]), "SY", 16)
                            inc(sy.dma_start(out=nmc_dbg, in_=nmcols[:, :]), "SY", 16)

                    for qc in range(4):
                        nkt = 4 * qc + 4

                        def emit_st(kt, first_of_pass):
                            off = max(0, (kt - 4 * qc) * 128)
                            diag = (kt >= 4 * qc)
                            par = kt % 2

                            @block.tensor
                            def _(t):
                                if first_of_pass:
                                    t.wait_ge(sem["SE"], C.get("SE"))
                                    t.wait_ge(sem["VE"], C.get("VE"))
                                    t.wait_ge(sem["SY"], C.get("SY"))
                                elif kt < 2:
                                    t.wait_ge(sem["SE"], C.get("SE"))
                                else:
                                    t.wait_ge(sem["SE"], C.get("SE") - 1)
                                for hh, pb in ((0, 2 + par), (1, 6 + par)):
                                    lo = hh * 64
                                    t.matmul(PS[pb][:, off:512],
                                             kt8[lo:lo + 64, kt * 128:(kt + 1) * 128],
                                             qt8[lo:lo + 64, qc * 512 + off: (qc + 1) * 512],
                                             start=True, stop=False, tile_position=(lo, 0))
                                    mm = t.matmul(PS[pb][:, off:512],
                                                  ones_bf[lo:lo + 1, 0:128],
                                                  negm[lo:lo + 1, hh * S + qc * 512 + off: hh * S + (qc + 1) * 512],
                                                  start=False, stop=not diag, tile_position=(lo, 0))
                                    if diag:
                                        mm = t.matmul(PS[pb][:, off:off + 128], ID, TRIL,
                                                      start=False, stop=True)
                                inc(mm, "MM")

                            @block.scalar
                            def _(s):
                                s.wait_ge(sem["MM"], C.get("MM"))
                                s.activation(expt[:, par * 1024 + off: par * 1024 + 512],
                                             PS[2 + par][:, off:512], AF.Exp)
                                inc(s.activation(expt[:, par * 1024 + 512 + off: par * 1024 + 1024],
                                                 PS[6 + par][:, off:512], AF.Exp), "SE")

                        emit_st(0, True)
                        for kt in range(nkt):
                            off = max(0, (kt - 4 * qc) * 128)
                            par = kt % 2
                            if kt + 1 < nkt:
                                emit_st(kt + 1, False)

                            @block.tensor
                            def _(t):
                                t.wait_ge(sem["SE"], C.get("SE") - (1 if kt + 1 < nkt else 0))
                                first = (kt == 0)
                                lastk = (kt == nkt - 1)
                                vb = kt * (NH * 65)
                                t.matmul(PS[4][0:65, off:512],
                                         vsb[:, vb + (2 * hp) * 65: vb + (2 * hp) * 65 + 65],
                                         expt[:, par * 1024 + off: par * 1024 + 512],
                                         start=first, stop=lastk)
                                inc(t.matmul(PS[5][0:65, off:512],
                                             vsb[:, vb + (2 * hp + 1) * 65: vb + (2 * hp + 1) * 65 + 65],
                                             expt[:, par * 1024 + 512 + off: par * 1024 + 1024],
                                             start=first, stop=lastk), "MM")

                        for hh in range(2):
                            @block.vector
                            def _(v):
                                v.wait_ge(sem["MM"], C.get("MM"))
                                inc(v.reciprocal(lnrow[0:1, 0:512], PS[4 + hh][64:65, 0:512]), "VE")
                                v.wait_ge(sem["VE"], C.get("VE"))
                                inc(v.tensor_copy(rrow[0:1, 0:512], lnrow[0:1, 0:512]), "VE")

                            @block.tensor
                            def _(t):
                                t.wait_ge(sem["VE"], C.get("VE"))
                                inc(t.matmul(PS[6][0:64, 0:512], ones_bf[0:1, 0:64],
                                             rrow[0:1, 0:512], start=True, stop=True), "MM")

                            @block.scalar
                            def _(s):
                                s.wait_ge(sem["MM"], C.get("MM"))
                                inc(s.copy(otmp[0:64, 0:512], PS[4 + hh][0:64, 0:512]), "SE")

                            @block.vector
                            def _(v):
                                v.wait_ge(sem["SE"], C.get("SE"))
                                inc(v.tensor_tensor(
                                    onorm[hh * 64:(hh + 1) * 64, hp * S + qc * 512: hp * S + (qc + 1) * 512],
                                    otmp[0:64, 0:512], PS[6][0:64, 0:512], op=ALU.mult), "VE")

                for tt in range(16):
                    @block.tensor
                    def _(t):
                        t.wait_ge(sem["VE"], C.get("VE"))
                        t.wait_ge(sem["SY"], C.get("SY"))
                        for dh in range(2):
                            for hp in range(8):
                                last = t.matmul(PS[dh][:, :],
                                                onorm[:, hp * S + tt * 128: hp * S + (tt + 1) * 128],
                                                pots[:, hp * 1024 + dh * 512: hp * 1024 + (dh + 1) * 512],
                                                start=(hp == 0), stop=(hp == 7))
                            inc(last, "MM")

                    @block.vector
                    def _(v):
                        v.wait_ge(sem["MM"], C.get("MM"))
                        v.tensor_copy(xnrow[:, 0:512], PS[0][:, :])
                        inc(v.tensor_copy(xnrow[:, 512:1024], PS[1][:, :]), "VE")

                    @block.sync
                    def _(sy):
                        sy.wait_ge(sem["VE"], C.get("VE"))
                        inc(sy.dma_start(out=attn_rows[b * S + tt * 128: b * S + (tt + 1) * 128, :],
                                         in_=xnrow[:, :]), "SY", 16)

            hard_barrier0()
            p01.close()
            _scope_leave(nc, "l0_main")
            _scope_enter(nc, "l0_ffn")

            with ExitStack() as pf:
                wgt = pf.enter_context(nc.sbuf_tensor("wgt", [128, 8 * 1024], BF16))
                wut = pf.enter_context(nc.sbuf_tensor("wut", [128, 8 * 1024], BF16))
                wdt = pf.enter_context(nc.sbuf_tensor("wdt", [128, 8 * 1024], BF16))
                xtc = pf.enter_context(nc.sbuf_tensor("xtc", [128, 8 * 512], BF16))
                silt = pf.enter_context(nc.sbuf_tensor("silt", [128, 512], BF16))
                hglu = pf.enter_context(nc.sbuf_tensor("hglu", [128, 8 * 512], BF16))
                prow = pf.enter_context(nc.sbuf_tensor("prow", [128, 1024], BF16))

                for og in range(3):
                    @block.sync
                    def _(sy):
                        sy.wait_ge(sem["MM"], C.get("MM"))
                        for dc in range(8):
                            inc(sy.dma_start(out=wgt[:, dc * 1024:(dc + 1) * 1024],
                                             in_=wg_b[og * 1024:(og + 1) * 1024,
                                                      dc * 128:(dc + 1) * 128],
                                             transpose=True), "SY", 16)
                            inc(sy.dma_start(out=wut[:, dc * 1024:(dc + 1) * 1024],
                                             in_=wu_b[og * 1024:(og + 1) * 1024,
                                                      dc * 128:(dc + 1) * 128],
                                             transpose=True), "SY", 16)
                            inc(sy.dma_start(out=wdt[:, dc * 1024:(dc + 1) * 1024],
                                             in_=wd_b[:, og * 1024 + dc * 128: og * 1024 + (dc + 1) * 128],
                                             transpose=True), "SY", 16)

                    for tc in range(8):
                        b_, t4 = tc // 4, tc % 4

                        @block.sync
                        def _(sy):
                            sy.wait_ge(sem["MM"], C.get("MM"))
                            inc(sy.dma_start(
                                out=xtc[:, :].rearrange("p (a s) -> p a s", a=8),
                                in_=xt0_dram[b_][:, :, t4 * 512:(t4 + 1) * 512]
                                .rearrange("a p s -> p a s")), "SY", 16)

                        for oc in range(8):
                            @block.tensor
                            def _(t):
                                t.wait_ge(sem["SY"], C.get("SY"))
                                t.wait_ge(sem["VE"], C.get("VE"))
                                for dc in range(8):
                                    mg = t.matmul(PS[0][:, :],
                                                  wgt[:, dc * 1024 + oc * 128: dc * 1024 + oc * 128 + 128],
                                                  xtc[:, dc * 512:(dc + 1) * 512],
                                                  start=(dc == 0), stop=(dc == 7))
                                inc(mg, "MM")
                                for dc in range(8):
                                    mu = t.matmul(PS[1][:, :],
                                                  wut[:, dc * 1024 + oc * 128: dc * 1024 + oc * 128 + 128],
                                                  xtc[:, dc * 512:(dc + 1) * 512],
                                                  start=(dc == 0), stop=(dc == 7))
                                inc(mu, "MM")

                            @block.scalar
                            def _(s):
                                s.wait_ge(sem["MM"], C.get("MM") - 1)
                                s.wait_ge(sem["VE"], C.get("VE"))
                                inc(s.activation(silt[:, :], PS[0][:, :], AF.Silu,
                                                 scale=1.0 / 127.0), "SE")

                            @block.vector
                            def _(v):
                                v.wait_ge(sem["MM"], C.get("MM"))
                                v.wait_ge(sem["SE"], C.get("SE"))
                                inc(v.scalar_tensor_tensor(hglu[:, oc * 512:(oc + 1) * 512],
                                                           PS[1][:, :], 1.0 / 127.0, silt[:, :],
                                                           op0=ALU.mult, op1=ALU.mult), "VE")

                        for ts_ in range(4):
                            @block.tensor
                            def _(t):
                                t.wait_ge(sem["VE"], C.get("VE"))
                                t.wait_ge(sem["SE"], C.get("SE"))
                                for dh in range(2):
                                    for oc in range(8):
                                        m = t.matmul(PS[2 + dh][:, :],
                                                     hglu[:, oc * 512 + ts_ * 128: oc * 512 + ts_ * 128 + 128],
                                                     wdt[:, oc * 1024 + dh * 512: oc * 1024 + (dh + 1) * 512],
                                                     start=(oc == 0), stop=(oc == 7))
                                    inc(m, "MM")

                            @block.scalar
                            def _(s):
                                s.wait_ge(sem["MM"], C.get("MM"))
                                s.wait_ge(sem["SY"], C.get("SY"))
                                s.activation(prow[:, 0:512], PS[2][:, :], AF.Copy, scale=1.0 / 127.0)
                                inc(s.activation(prow[:, 512:1024], PS[3][:, :], AF.Copy,
                                                 scale=1.0 / 127.0), "SE")

                            @block.sync
                            def _(sy):
                                sy.wait_ge(sem["SE"], C.get("SE"))
                                r0 = b_ * S + t4 * 512 + ts_ * 128
                                inc(sy.dma_start(out=ffnp[og, r0:r0 + 128, :], in_=prow[:, :]),
                                    "SY", 16)

            hard_barrier0()
            _scope_leave(nc, "l0_ffn")
            _scope_enter(nc, "l0_asm")

            with ExitStack() as pa:
                xrowf = pa.enter_context(nc.sbuf_tensor("xrowf", [128, 1024], F32))
                arow = pa.enter_context(nc.sbuf_tensor("arow", [128, 1024], BF16))
                p0r = pa.enter_context(nc.sbuf_tensor("p0r", [128, 1024], BF16))
                p1r = pa.enter_context(nc.sbuf_tensor("p1r", [128, 1024], BF16))
                p2r = pa.enter_context(nc.sbuf_tensor("p2r", [128, 1024], BF16))
                x1t = pa.enter_context(nc.sbuf_tensor("x1t", [128, 1024], F32))

                for tt in range(TOK0 // 128):
                    b_ = tt // 16

                    @block.sync
                    def _(sy):
                        sy.wait_ge(sem["VE"], C.get("VE"))
                        r0 = tt * 128
                        inc(sy.dma_start(out=xrowf[:, :],
                                         in_=x_in[b_, (tt % 16) * 128:(tt % 16) * 128 + 128, :]),
                            "SY", 16)
                        inc(sy.dma_start(out=arow[:, :], in_=attn_rows[r0:r0 + 128, :]), "SY", 16)
                        for g_, dst in ((0, p0r), (1, p1r), (2, p2r)):
                            inc(sy.dma_start(out=dst[:, :], in_=ffnp[g_, r0:r0 + 128, :]), "SY", 16)

                    @block.vector
                    def _(v):
                        v.wait_ge(sem["SY"], C.get("SY"))
                        v.tensor_tensor(x1t[:, :], xrowf[:, :], arow[:, :], op=ALU.add)
                        v.tensor_tensor(x1t[:, :], x1t[:, :], p0r[:, :], op=ALU.add)
                        v.tensor_tensor(x1t[:, :], x1t[:, :], p1r[:, :], op=ALU.add)
                        inc(v.tensor_tensor(x1t[:, :], x1t[:, :], p2r[:, :], op=ALU.add), "VE")

                    @block.sync
                    def _(sy):
                        sy.wait_ge(sem["VE"], C.get("VE"))
                        s0 = (tt % 16) * 128
                        dest = s0 // SPC
                        off = dest * (BPC * SPC) + b_ * SPC + (s0 % SPC)
                        inc(sy.dma_start(out=a2a_in[off:off + 128, :], in_=x1t[:, :]), "SY", 16)
                        if l0_only:
                            inc(sy.dma_start(out=x1_out[tt * 128:(tt + 1) * 128, :],
                                             in_=x1t[:, :]), "SY", 16)

            @block.gpsimd
            def _(g):
                g.wait_ge(sem["GP"], C.get("GP"))
                g.wait_ge(sem["SY"], C.get("SY"))
                if l0_only:
                    inc(g.dma_start(out=attn_dbg, in_=attn_rows.ap()), "GP", 16)
                    inc(g.dma_start(out=ffnp_dbg, in_=ffnp.ap()), "GP", 16)
                    inc(g.dma_start(out=xt0_dbg, in_=xt0_dram.ap()), "GP", 16)
                    g.wait_ge(sem["GP"], C.get("GP"))

        if l0_only:
            return nc

        if not l1_only:
            _scope_leave(nc, "l0_asm")
        emit_l1(nc, block, sem, C, inc, PS, top, locals())
    return nc

BIGT = 8192.0



def emit_l1_casts(nc, block, sem, C, inc, env):
    moe1, moe2, moe3 = env["moe1"], env["moe2"], env["moe3"]
    moe1_my, moe2_my, moe3_my = env["moe1_my"], env["moe2_my"], env["moe3_my"]

    @block.gpsimd
    def _(g):
        g.wait_ge(sem["GP"], C.get("GP"))
        inc(g.dma_start(out=env["wqkv_b"].ap(), in_=env["wqkv_in"]), "GP", 16)
        inc(g.dma_start(out=env["sg_b"].ap(), in_=env["sg_in"]), "GP", 16)
        inc(g.dma_start(out=env["su_b"].ap(), in_=env["su_in"]), "GP", 16)
        inc(g.dma_start(out=env["sd_b"].ap(), in_=env["sd_in"]), "GP", 16)
        inc(g.dma_start(out=env["gw_bd"].ap(), in_=env["gw_in"]), "GP", 16)
        inc(g.dma_start(out=moe1_my.ap(), in_=env["w1_my"]), "GP", 16)
        inc(g.dma_start(out=moe3_my.ap(), in_=env["w3_my"]), "GP", 16)
        inc(g.dma_start(out=moe2_my.ap(), in_=env["w2_my"]), "GP", 16)
        g.wait_ge(sem["GP"], C.get("GP"))
        for my, full in ((moe1_my, moe1), (moe3_my, moe3), (moe2_my, moe2)):
            inc(g.collective_compute(
                "AllGather", ALU.bypass, replica_groups=[list(range(NCORES))],
                ins=[my.ap().rearrange("e a b -> (e a) b")],
                outs=[full.ap().rearrange("e a b -> (e a) b")]), "CC")


def emit_l1(nc, block, sem, C, inc, PS, top, env):
    l1_only = env["l1_only"]
    a2a_out = env["a2a_out"]
    ID, idf_sb, ones_bf, epscol = env["ID"], env["idf_sb"], env["ones_bf"], env["epscol"]
    y_out = env["y_out"]
    xn1tab_h = env["xn1tab"]
    yetab, part1 = env["yetab"], env["part1"]
    cnt_bounce, cnt_all = env["cnt_bounce"], env["cnt_all"]
    moe1, moe2, moe3 = env["moe1"], env["moe2"], env["moe3"]
    moe1_my, moe2_my, moe3_my = env["moe1_my"], env["moe2_my"], env["moe3_my"]
    wqkv_b, sg_b, su_b, sd_b, gw_bd = (env["wqkv_b"], env["sg_b"], env["su_b"],
                                       env["sd_b"], env["gw_bd"])

    def vw(v):
        v.wait_ge(sem["VE"], C.get("VE"))

    def hard_barrier():
        for engname in ("tensor", "vector", "scalar", "sync", "gpsimd"):
            def _bar(eng):
                for s_ in ("SY", "GP", "MM", "VE", "SE", "CC", "ST"):
                    eng.wait_ge(sem[s_], C.get(s_))
            getattr(block, engname)(_bar)

    _scope_enter(nc, "l1_prep")
    if not l1_only:
        a2a_in = env["a2a_in"]

        @block.gpsimd
        def _(g):
            g.wait_ge(sem["GP"], C.get("GP"))
            g.wait_ge(sem["SY"], C.get("SY"))
            with allow_ici_all_to_all():
                inc(g.collective_compute("AllToAll", ALU.bypass,
                                         replica_groups=[list(range(NCORES))],
                                         ins=[a2a_in.ap()], outs=[a2a_out]), "CC")
            g.wait_ge(sem["CC"], C.get("CC"))

    if l1_only:
        emit_l1_casts(nc, block, sem, C, inc, env)

    l1c = ExitStack()
    xn1t = l1c.enter_context(nc.sbuf_tensor("xn1t", [128, 8 * TOK1], BF16))
    onesf = l1c.enter_context(nc.sbuf_tensor("onesf", [1, 128], F32))
    onescf = l1c.enter_context(nc.sbuf_tensor("onescf", [128, 1], F32))
    utile_s = l1c.enter_context(nc.sbuf_tensor("utile_s", [128, 128], F32))
    uplain_s = l1c.enter_context(nc.sbuf_tensor("uplain_s", [128, 128], F32))
    ebm_s = l1c.enter_context(nc.sbuf_tensor("ebm_s", [128, 16], F32))
    ebt_s = l1c.enter_context(nc.sbuf_tensor("ebt_s", [16, 128], F32))
    u1_s = l1c.enter_context(nc.sbuf_tensor("u1_s", [128, 16], F32))
    iota_s = l1c.enter_context(nc.sbuf_tensor("iota_s", [128, 1], F32))
    bd16_s = l1c.enter_context(nc.sbuf_tensor("bd16_s", [128, 512], BF16))
    ebias_bc = l1c.enter_context(nc.sbuf_tensor("ebias_bc", [128, 64], F32))
    e192p1 = l1c.enter_context(nc.sbuf_tensor("e192p1", [128, 64], F32))
    gammac = l1c.enter_context(nc.sbuf_tensor("gammac", [128, 16], F32))
    c2c = l1c.enter_context(nc.sbuf_tensor("c2c", [128, 16], F32))
    qkb2 = l1c.enter_context(nc.sbuf_tensor("qkb2", [128, 16], F32))
    vbias = l1c.enter_context(nc.sbuf_tensor("vbias", [128, 1024], F32))
    gwT = l1c.enter_context(nc.sbuf_tensor("gwT", [128, 512], BF16))
    destbuf = l1c.enter_context(nc.sbuf_tensor("destbuf", [128, 256], I32))
    gidxT0 = l1c.enter_context(nc.sbuf_tensor("gidxT0", [128, 64], I32))
    gidxT1 = l1c.enter_context(nc.sbuf_tensor("gidxT1", [128, 64], I32))
    zrow = l1c.enter_context(nc.sbuf_tensor("zrow", [1, D + E], BF16))
    srow = l1c.enter_context(nc.sbuf_tensor("srow", [16, 128], F32))
    brow = l1c.enter_context(nc.sbuf_tensor("brow", [1, 1120], F32))

    @block.sync
    def _(sy):
        inc(sy.dma_start(out=utile_s[:, :], in_=env["u_tile"]), "SY", 16)
        inc(sy.dma_start(out=uplain_s[:, :], in_=env["u_plain"]), "SY", 16)
        inc(sy.dma_start(out=ebm_s[:, :], in_=env["eb_mat"]), "SY", 16)
        inc(sy.dma_start(out=ebt_s[:, :], in_=env["ebt_mat"]), "SY", 16)
        inc(sy.dma_start(out=u1_s[:, :], in_=env["u1_mine"]), "SY", 16)
        inc(sy.dma_start(out=iota_s[:, :], in_=env["iota_col"]), "SY", 16)
        inc(sy.dma_start(out=bd16_s[:, :], in_=env["bd16_mask"]), "SY", 16)
        inc(sy.dma_start(out=brow[0:1, 0:64], in_=env["e192_row"]), "SY", 16)
        inc(sy.dma_start(out=srow[:, :],
                         in_=env["bqkv_in"][0:1, :]
                         .rearrange("o (h c) -> (o h) c", h=16)[:, 0:128]), "SY", 16)

    @block.vector
    def _(v):
        v.wait_ge(sem["SY"], C.get("SY"))
        v.memset(onesf[:, :], 1.0)
        v.memset(onescf[:, :], 1.0)
        inc(v.memset(zrow[:, :], 0.0), "VE")

    @block.gpsimd
    def _(g):
        g.wait_ge(sem["VE"], C.get("VE"))
        inc(g.dma_start(out=xn1tab_h.ap()[0:1, :], in_=zrow[0:1, :]), "GP", 16)
        inc(g.dma_start(out=yetab.ap()[0:1, :], in_=zrow[0:1, 0:D]), "GP", 16)

    @block.tensor
    def _(t):
        t.wait_ge(sem["VE"], C.get("VE"))
        inc(t.matmul(PS[5][:, 0:64], onesf[0:1, :], brow[0:1, 0:64],
                     start=True, stop=True), "MM")
        inc(t.transpose(PS[6][0:128, 0:16], srow[:, :], idf_sb[0:16, 0:16]), "MM")

    @block.vector
    def _(v):
        v.wait_ge(sem["MM"], C.get("MM"))
        inc(v.tensor_copy(e192p1[:, :], PS[5][:, 0:64]), "VE")
        inc(v.tensor_copy(qkb2[:, :], PS[6][0:128, 0:16]), "VE")

    @block.sync
    def _(sy):
        sy.wait_ge(sem["MM"], C.get("MM"))
        inc(sy.dma_start(out=brow[0:1, 0:64], in_=env["ebias_in"]), "SY", 16)
        inc(sy.dma_start(out=brow[0:1, 64:80], in_=env["gamma_in"]), "SY", 16)
        inc(sy.dma_start(out=brow[0:1, 80:96], in_=env["beta_in"]), "SY", 16)
        inc(sy.dma_start(
            out=brow[0:1, 96:608].rearrange("o (h c) -> o h c", h=8),
            in_=env["bqkv_in"][0:1, :]
            .rearrange("o (h c) -> o h c", h=16)[:, 0:8, 128:192]), "SY", 16)
        inc(sy.dma_start(
            out=brow[0:1, 608:1120].rearrange("o (h c) -> o h c", h=8),
            in_=env["bqkv_in"][0:1, :]
            .rearrange("o (h c) -> o h c", h=16)[:, 8:16, 128:192]), "SY", 16)
        sy.wait_ge(sem["GP"], C.get("GP"))
        for dc in range(8):
            inc(sy.dma_start(out=gwT[:, dc * 64:(dc + 1) * 64],
                             in_=gw_bd.ap()[:, dc * 128:(dc + 1) * 128], transpose=True),
                "SY", 16)

    @block.tensor
    def _(t):
        t.wait_ge(sem["SY"], C.get("SY"))
        t.wait_ge(sem["VE"], C.get("VE"))
        inc(t.matmul(PS[5][:, 0:64], onesf[0:1, :], brow[0:1, 0:64],
                     start=True, stop=True), "MM")
        inc(t.matmul(PS[6][:, 0:16], onesf[0:1, :], brow[0:1, 64:80],
                     start=True, stop=True), "MM")
        inc(t.matmul(PS[6][:, 16:32], onesf[0:1, :], brow[0:1, 80:96],
                     start=True, stop=True), "MM")
        inc(t.matmul(PS[7][:, 0:512], onesf[0:1, :], brow[0:1, 96:608],
                     start=True, stop=True), "MM")
        inc(t.matmul(PS[4][:, 0:512], onesf[0:1, :], brow[0:1, 608:1120],
                     start=True, stop=True), "MM")

    @block.vector
    def _(v):
        v.wait_ge(sem["MM"], C.get("MM"))
        inc(v.tensor_copy(ebias_bc[:, :], PS[5][:, 0:64]), "VE")
        inc(v.tensor_copy(gammac[:, :], PS[6][:, 0:16]), "VE")
        inc(v.tensor_copy(vbias[:, 0:512], PS[7][:, 0:512]), "VE")
        inc(v.tensor_copy(vbias[:, 512:1024], PS[4][:, 0:512]), "VE")
        vw(v)
        inc(v.tensor_scalar(c2c[:, :], gammac[:, :], 0.5, None, op0=ALU.mult), "VE")
        vw(v)
        inc(v.tensor_tensor(c2c[:, :], c2c[:, :], PS[6][:, 16:32], op=ALU.add), "VE")

    _scope_leave(nc, "l1_prep")
    _scope_enter(nc, "l1_ln")
    p45 = ExitStack()
    lnrow = p45.enter_context(nc.sbuf_tensor("l1_lnrow", [128, 1024], F32))
    lnsc = p45.enter_context(nc.sbuf_tensor("l1_lnsc", [128, 8], F32))
    xnrow = p45.enter_context(nc.sbuf_tensor("l1_xnrow", [128, 1024], BF16))
    lnrow2 = p45.enter_context(nc.sbuf_tensor("l1_lnrow2", [128, 1024], F32))
    lnsc2 = p45.enter_context(nc.sbuf_tensor("l1_lnsc2", [128, 8], F32))
    xnrow2 = p45.enter_context(nc.sbuf_tensor("l1_xnrow2", [128, 1024], BF16))
    x1v = a2a_out.rearrange("(b s) d -> b s d", b=B)

    marks4 = {}
    marks4g = {}
    se4 = {}
    for tt in range(32):
        LR = lnrow if tt % 2 == 0 else lnrow2
        LS = lnsc if tt % 2 == 0 else lnsc2
        XR = xnrow if tt % 2 == 0 else xnrow2
        PB = 6 + tt % 2

        @block.sync
        def _(sy):
            if tt >= 2:
                sy.wait_ge(sem["VE"], marks4[tt - 2])
                sy.wait_ge(sem["GP"], C.get("GP"))
            else:
                sy.wait_ge(sem["MM"], C.get("MM"))
            if tt >= 1:
                sy.wait_ge(sem["SE"], se4[tt - 1])
            for ss in range(8):
                inc(sy.dma_start(out=LR[ss * 16:(ss + 1) * 16, :],
                                 in_=x1v[:, 8 * tt + ss, :]), "SY", 16)

        @block.scalar
        def _(s):
            s.wait_ge(sem["SY"], C.get("SY"))
            s.activation(XR[:, :], LR[:, :], AF.Copy, scale=1.0 / D,
                         accum_out=LS[:, 1:2])
            se4[tt] = inc(s.activation(XR[:, :], LR[:, :], AF.Square,
                             scale=1.0 / float(np.sqrt(D)), accum_out=LS[:, 2:3]), "SE")

        @block.vector
        def _(v):
            v.wait_ge(sem["SE"], C.get("SE"))
            inc(v.tensor_tensor(LS[:, 3:4], LS[:, 1:2], LS[:, 1:2], op=ALU.mult), "VE")
            vw(v)
            inc(v.tensor_tensor(LS[:, 5:6], LS[:, 2:3], LS[:, 3:4], op=ALU.subtract), "VE")

        @block.scalar
        def _(s):
            s.wait_ge(sem["VE"], C.get("VE"))
            inc(s.activation(LS[:, 7:8], LS[:, 5:6], AF.Ln, bias=epscol[:, 0:1]), "SE")
            s.wait_ge(sem["SE"], C.get("SE"))
            inc(s.activation(LS[:, 0:1], LS[:, 7:8], AF.Exp, scale=-0.5), "SE")

        @block.vector
        def _(v):
            v.wait_ge(sem["SE"], C.get("SE"))
            inc(v.tensor_scalar(XR[:, :], LR[:, :], LS[:, 1:2], LS[:, 0:1],
                                op0=ALU.subtract, op1=ALU.mult), "VE")

        @block.gpsimd
        def _(g):
            g.wait_ge(sem["VE"], C.get("VE"))
            marks4g[tt] = inc(
                g.dma_start(out=xn1tab_h.ap()[1 + tt * 128: 1 + (tt + 1) * 128, 0:D],
                            in_=XR[:, :]), "GP", 16)

        @block.tensor
        def _(t):
            t.wait_ge(sem["VE"], C.get("VE"))
            for dc in range(8):
                last = t.transpose(PS[PB][:, :].bitcast(BF16)[:, dc * 128:(dc + 1) * 128],
                                   XR[:, dc * 128:(dc + 1) * 128], ID)
            inc(last, "MM")

        @block.vector
        def _(v):
            v.wait_ge(sem["MM"], C.get("MM"))
            marks4[tt] = inc(v.tensor_copy(
                xn1t[:, :].rearrange("p (a s) -> p a s", a=8)[:, :, tt * 128:(tt + 1) * 128],
                PS[PB][:, :].bitcast(BF16).rearrange("p (a s) -> p a s", a=8)), "VE")

    _scope_leave(nc, "l1_ln")
    _scope_enter(nc, "l1_route")
    sco = p45.enter_context(nc.sbuf_tensor("sco", [128, 64], F32))
    sfc = p45.enter_context(nc.sbuf_tensor("sfc", [128, 64], F32))
    g8buf = p45.enter_context(nc.sbuf_tensor("g8buf", [128, 64], F32))
    gs16 = p45.enter_context(nc.sbuf_tensor("gs16", [128, 16], F32))
    gmi = p45.enter_context(nc.sbuf_tensor("gmi", [128, 8], I32))
    gmf = p45.enter_context(nc.sbuf_tensor("gmf", [128, 8], F32))
    tmp8 = p45.enter_context(nc.sbuf_tensor("tmp8", [128, 64], F32))
    t8v = p45.enter_context(nc.sbuf_tensor("t8v", [128, 8], F32))
    zap = p45.enter_context(nc.sbuf_tensor("zap", [128, 64], F32))
    ai32 = p45.enter_context(nc.sbuf_tensor("ai32", [128, 64], I32))
    afbuf = p45.enter_context(nc.sbuf_tensor("afbuf", [128, 2048], F32))
    wnb = p45.enter_context(nc.sbuf_tensor("wnb", [128, 64], BF16))
    wnf = p45.enter_context(nc.sbuf_tensor("wnf", [128, 64], F32))

    for tt in range(32):
        @block.tensor
        def _(t):
            t.wait_ge(sem["SE"], C.get("SE"))
            t.wait_ge(sem["VE"], C.get("VE"))
            for dc in range(8):
                mm = t.matmul(PS[5][:, 0:64],
                              xn1t[:, dc * TOK1 + tt * 128: dc * TOK1 + (tt + 1) * 128],
                              gwT[:, dc * 64:(dc + 1) * 64],
                              start=(dc == 0), stop=(dc == 7))
            inc(mm, "MM")

        @block.scalar
        def _(s):
            s.wait_ge(sem["MM"], C.get("MM"))
            s.wait_ge(sem["VE"], C.get("VE"))
            inc(s.activation(sco[:, :], PS[5][:, 0:64], AF.Sigmoid), "SE")

        @block.vector
        def _(v):
            v.wait_ge(sem["SE"], C.get("SE"))
            v.wait_ge(sem["GP"], C.get("GP"))
            inc(v.tensor_tensor(sfc[:, :], sco[:, :], ebias_bc[:, :], op=ALU.add), "VE")
            vw(v)
            for g_ in range(8):
                last = v.max(out=g8buf[:, 8 * g_:8 * g_ + 8], in_=sfc[:, 8 * g_:8 * g_ + 8])
            inc(last, "VE")
            vw(v)
            for g_ in range(8):
                last = v.reduce_sum(gs16[:, g_:g_ + 1], g8buf[:, 8 * g_:8 * g_ + 2], axis=AX)
            inc(last, "VE")
            vw(v)
            inc(v.max(out=gs16[:, 8:16], in_=gs16[:, 0:8]), "VE")
            vw(v)
            for g_ in range(8):
                last = v.tensor_tensor(gmi[:, g_:g_ + 1], gs16[:, g_:g_ + 1],
                                       gs16[:, 11:12], op=ALU.is_ge)
            inc(last, "VE")
            vw(v)
            inc(v.tensor_copy(gmf[:, :], gmi[:, :]), "VE")
            vw(v)
            for g_ in range(8):
                last = v.tensor_scalar(tmp8[:, 8 * g_:8 * g_ + 8], sfc[:, 8 * g_:8 * g_ + 8],
                                       gmf[:, g_:g_ + 1], None, op0=ALU.mult)
            inc(last, "VE")
            vw(v)
            inc(v.max(out=t8v[:, :], in_=tmp8[:, :]), "VE")
            vw(v)
            inc(v.match_replace(out=zap[:, :], in_to_replace=t8v[:, :],
                                in_values=tmp8[:, :], imm_value=NEG), "VE")
            vw(v)
            inc(v.tensor_scalar(ai32[:, :], zap[:, :], -1e29, None, op0=ALU.is_le), "VE")
            vw(v)
            inc(v.tensor_copy(afbuf[:, tt * 64:(tt + 1) * 64], ai32[:, :]), "VE")
            vw(v)
            inc(v.tensor_tensor(wnf[:, :], afbuf[:, tt * 64:(tt + 1) * 64], sco[:, :],
                                op=ALU.mult), "VE")
            vw(v)
            inc(v.reduce_sum(lnsc[:, 4:5], wnf[:, :], axis=AX), "VE")
            vw(v)
            inc(v.tensor_scalar(lnsc[:, 5:6], lnsc[:, 4:5], 1e-20, None, op0=ALU.add), "VE")
            vw(v)
            inc(v.reciprocal(lnsc[:, 6:7], lnsc[:, 5:6]), "VE")
            vw(v)
            inc(v.tensor_scalar(wnb[:, :], wnf[:, :], lnsc[:, 6:7], 2.5,
                                op0=ALU.mult, op1=ALU.mult), "VE")

        @block.gpsimd
        def _(g):
            g.wait_ge(sem["VE"], C.get("VE"))
            inc(g.dma_start(out=xn1tab_h.ap()[1 + tt * 128: 1 + (tt + 1) * 128, D:D + E],
                            in_=wnb[:, :]), "GP", 16)

        @block.tensor
        def _(t):
            t.wait_ge(sem["VE"], C.get("VE"))
            inc(t.matmul(PS[6][0:16, 0:64], ebm_s[:, :], afbuf[:, tt * 64:(tt + 1) * 64],
                         start=(tt == 0), stop=(tt == 31)), "MM")

    cnt16 = p45.enter_context(nc.sbuf_tensor("cnt16", [16, 64], F32))
    gb16 = p45.enter_context(nc.sbuf_tensor("gb16", [16, 64], F32))
    gbr = p45.enter_context(nc.sbuf_tensor("gbr", [16, 64], F32))
    runb = p45.enter_context(nc.sbuf_tensor("runb", [16, 64], F32))
    runk = p45.enter_context(nc.sbuf_tensor("runk", [1, 64], F32))
    cntall = p45.enter_context(nc.sbuf_tensor("cntall", [128, 64], F32))

    @block.vector
    def _(v):
        v.wait_ge(sem["MM"], C.get("MM"))
        inc(v.tensor_copy(cnt16[:, :], PS[6][0:16, 0:64]), "VE")
        v.memset(runb[:, :], 0.0)
        inc(v.memset(runk[:, :], 0.0), "VE")

    @block.gpsimd
    def _(g):
        g.wait_ge(sem["VE"], C.get("VE"))
        inc(g.dma_start(out=cnt_bounce.ap(), in_=cnt16[:, :]), "GP", 16)
        g.wait_ge(sem["GP"], C.get("GP"))
        inc(g.collective_compute("AllGather", ALU.bypass,
                                 replica_groups=[list(range(NCORES))],
                                 ins=[cnt_bounce.ap()], outs=[cnt_all.ap()]), "CC")
        g.wait_ge(sem["CC"], C.get("CC"))
        inc(g.dma_start(out=cntall[:, :], in_=cnt_all.ap()), "GP", 16)

    @block.tensor
    def _(t):
        t.wait_ge(sem["GP"], C.get("GP"))
        inc(t.matmul(PS[5][0:16, 0:64], u1_s[:, :], cntall[:, :], start=True, stop=True),
            "MM")

    @block.vector
    def _(v):
        v.wait_ge(sem["MM"], C.get("MM"))
        inc(v.tensor_copy(gb16[:, :], PS[5][0:16, 0:64]), "VE")

    kint = p45.enter_context(nc.sbuf_tensor("kint", [128, 192], I32))
    keepf = p45.enter_context(nc.sbuf_tensor("keepf", [128, 64], F32))
    tmpc = p45.enter_context(nc.sbuf_tensor("tmpc", [128, 64], F32))
    fcol = p45.enter_context(nc.sbuf_tensor("fcol", [128, 1], F32))
    sval = p45.enter_context(nc.sbuf_tensor("sval", [128, 64], F32))
    evT = p45.enter_context(nc.sbuf_tensor("evT", [64, TOK1], F32))
    d8 = p45.enter_context(nc.sbuf_tensor("d8", [128, 8], F32))
    d8b = p45.enter_context(nc.sbuf_tensor("d8b", [128, 8], F32))
    m8i = p45.enter_context(nc.sbuf_tensor("m8i", [128, 8], I32))
    m8f = p45.enter_context(nc.sbuf_tensor("m8f", [128, 8], F32))

    for tt in range(32):
        @block.vector
        def _(v):
            vw(v)
            inc(v.tensor_tensor(gbr[:, :], gb16[:, :], runb[:, :], op=ALU.add), "VE")
            vw(v)

        @block.tensor
        def _(t):
            t.wait_ge(sem["VE"], C.get("VE"))
            af = afbuf[:, tt * 64:(tt + 1) * 64]
            t.matmul(PS[4][:, 0:64], utile_s[:, :], af, start=True, stop=False)
            inc(t.matmul(PS[4][:, 0:64], ebt_s[:, :], gbr[:, :], start=False, stop=True),
                "MM")
            inc(t.matmul(PS[2][0:16, 0:64], ebm_s[:, :], af, start=True, stop=True), "MM")

        @block.vector
        def _(v):
            v.wait_ge(sem["MM"], C.get("MM"))
            inc(v.tensor_scalar(kint[:, 0:64], PS[4][:, 0:64], float(CAP), None,
                                op0=ALU.is_lt), "VE")
            vw(v)
            inc(v.tensor_copy(keepf[:, :], kint[:, 0:64]), "VE")
            vw(v)
            inc(v.tensor_tensor(keepf[:, :], keepf[:, :],
                                afbuf[:, tt * 64:(tt + 1) * 64], op=ALU.mult), "VE")
            vw(v)
            inc(v.tensor_tensor(runb[:, :], runb[:, :], PS[2][0:16, 0:64], op=ALU.add), "VE")
            vw(v)

        @block.tensor
        def _(t):
            t.wait_ge(sem["VE"], C.get("VE"))
            t.matmul(PS[3][:, 0:64], uplain_s[:, :], keepf[:, :], start=True, stop=False)
            inc(t.matmul(PS[3][:, 0:64], onesf[0:1, :], runk[0:1, :], start=False,
                         stop=True), "MM")
            inc(t.matmul(PS[1][0:1, 0:64], onescf[:, :], keepf[:, :], start=True,
                         stop=True), "MM")

        @block.vector
        def _(v):
            v.wait_ge(sem["MM"], C.get("MM"))
            inc(v.tensor_scalar(fcol[:, :], iota_s[:, :], -1.0, BIGT - 1.0 - 128.0 * tt,
                                op0=ALU.mult, op1=ALU.add), "VE")
            vw(v)
            inc(v.tensor_scalar(sval[:, :], keepf[:, :], fcol[:, 0:1], None,
                                op0=ALU.mult), "VE")
            inc(v.scalar_tensor_tensor(tmpc[:, :], PS[3][:, 0:64], 1.0, e192p1[:, :],
                                       op0=ALU.mult, op1=ALU.add), "VE")
            vw(v)
            inc(v.tensor_scalar(tmpc[:, :], tmpc[:, :], -1.0, EBIG,
                                op0=ALU.mult, op1=ALU.add), "VE")
            vw(v)
            inc(v.tensor_tensor(tmpc[:, :], tmpc[:, :], keepf[:, :], op=ALU.mult), "VE")
            vw(v)
            inc(v.max(out=d8[:, :], in_=tmpc[:, :]), "VE")
            vw(v)
            inc(v.tensor_scalar(d8b[:, :], d8[:, :], -1.0, EBIG, op0=ALU.mult,
                                op1=ALU.add), "VE")
            inc(v.tensor_scalar(m8i[:, :], d8[:, :], 0.0, None, op0=ALU.is_gt), "VE")
            vw(v)
            inc(v.tensor_copy(m8f[:, :], m8i[:, :]), "VE")
            vw(v)
            inc(v.tensor_tensor(d8b[:, :], d8b[:, :], m8f[:, :], op=ALU.mult), "VE")
            vw(v)
            inc(v.tensor_copy(destbuf[:, tt * 8:(tt + 1) * 8], d8b[:, :]), "VE")
            inc(v.tensor_tensor(runk[0:1, :], runk[0:1, :], PS[1][0:1, 0:64],
                                op=ALU.add), "VE")
            vw(v)

        @block.tensor
        def _(t):
            t.wait_ge(sem["VE"], C.get("VE"))
            inc(t.transpose(PS[7][0:64, 0:128], sval[:, :], idf_sb[:, :]), "MM")

        @block.vector
        def _(v):
            v.wait_ge(sem["MM"], C.get("MM"))
            inc(v.tensor_copy(evT[:, tt * 128:(tt + 1) * 128], PS[7][0:64, 0:128]), "VE")

    cand = p45.enter_context(nc.sbuf_tensor("cand", [64, 320], F32))
    gext = p45.enter_context(nc.sbuf_tensor("gext", [64, 192], F32))
    gfl = p45.enter_context(nc.sbuf_tensor("gfl", [64, 384], F32))

    @block.vector
    def _(v):
        vw(v)
        for blk in range(8):
            for r in range(5):
                inc(v.max(out=cand[:, blk * 40 + r * 8: blk * 40 + (r + 1) * 8],
                          in_=evT[:, blk * 512:(blk + 1) * 512]), "VE")
                vw(v)
                if r < 4:
                    inc(v.match_replace(out=evT[:, blk * 512:(blk + 1) * 512],
                                        in_to_replace=cand[:, blk * 40 + r * 8: blk * 40 + (r + 1) * 8],
                                        in_values=evT[:, blk * 512:(blk + 1) * 512],
                                        imm_value=0.0), "VE")
                    vw(v)
        for r in range(24):
            inc(v.max(out=gext[:, r * 8:(r + 1) * 8], in_=cand[:, :]), "VE")
            vw(v)
            if r < 23:
                inc(v.match_replace(out=cand[:, :], in_to_replace=gext[:, r * 8:(r + 1) * 8],
                                    in_values=cand[:, :], imm_value=0.0), "VE")
                vw(v)
        inc(v.tensor_scalar(gfl[:, 0:192], gext[:, :], -1.0, BIGT, op0=ALU.mult,
                            op1=ALU.add), "VE")
        inc(v.tensor_scalar(kint[0:64, 0:192], gext[:, :], 0.0, None, op0=ALU.is_gt), "VE")
        vw(v)
        inc(v.tensor_copy(gfl[:, 192:384], kint[0:64, 0:192]), "VE")
        vw(v)
        inc(v.tensor_tensor(gfl[:, 0:192], gfl[:, 0:192], gfl[:, 192:384], op=ALU.mult),
            "VE")
        vw(v)

    @block.tensor
    def _(t):
        t.wait_ge(sem["VE"], C.get("VE"))
        inc(t.transpose(PS[7][0:128, 0:64], gfl[:, 0:128], idf_sb[0:64, 0:64]), "MM")
        inc(t.transpose(PS[6][0:64, 0:64], gfl[:, 128:192], idf_sb[0:64, 0:64]), "MM")

    @block.vector
    def _(v):
        v.wait_ge(sem["MM"], C.get("MM"))
        inc(v.tensor_copy(gidxT0[:, :], PS[7][0:128, 0:64]), "VE")
        v.memset(gidxT1[:, :], 0)
        vw(v)
        inc(v.tensor_copy(gidxT1[0:64, :], PS[6][0:64, 0:64]), "VE")

    if l1_only:
        @block.gpsimd
        def _(g):
            g.wait_ge(sem["VE"], C.get("VE"))
            inc(g.dma_start(out=env["dbg_af"], in_=afbuf[:, :]), "GP", 16)
            inc(g.dma_start(out=env["dbg_cnt"][0:16, 0:64], in_=cnt16[:, :]), "GP", 16)
            inc(g.dma_start(out=env["dbg_cnt"][16:32, 64:128], in_=gb16[:, :]), "GP", 16)
            inc(g.dma_start(out=env["dbg_cnt"][0:128, 128:192], in_=cntall[:, :]), "GP", 16)

    hard_barrier()
    p45.close()
    _scope_leave(nc, "l1_route")
    _scope_enter(nc, "l1_attn")

    p6 = ExitStack()
    qkwT = p6.enter_context(nc.sbuf_tensor("qkwT", [128, NH * 8 * 128], BF16))
    vwT = p6.enter_context(nc.sbuf_tensor("vwT", [128, 8 * 1024], BF16))
    qT = p6.enter_context(nc.sbuf_tensor("qT", [64, NH * 512], BF16))
    kT = p6.enter_context(nc.sbuf_tensor("kT", [64, NH * 512], BF16))
    vtok = p6.enter_context(nc.sbuf_tensor("vtok", [128, 4 * 1024], BF16))
    xb6 = p6.enter_context(nc.sbuf_tensor("xb6", [128, 512], BF16))
    x26 = p6.enter_context(nc.sbuf_tensor("x26", [128, 512], BF16))
    i16 = p6.enter_context(nc.sbuf_tensor("i16", [128, 512], BF16))
    i26 = p6.enter_context(nc.sbuf_tensor("i26", [128, 512], BF16))
    probsb = p6.enter_context(nc.sbuf_tensor("probsb", [128, 512], BF16))
    a1rows = p6.enter_context(nc.sbuf_tensor("a1rows", [128, 4 * 1024], BF16))

    @block.sync
    def _(sy):
        sy.wait_ge(sem["GP"], C.get("GP"))
        for h in range(NH):
            for dc in range(8):
                inc(sy.dma_start(out=qkwT[:, (h * 8 + dc) * 128:(h * 8 + dc + 1) * 128],
                                 in_=wqkv_b.ap()[192 * h:192 * h + 128,
                                                 dc * 128:(dc + 1) * 128],
                                 transpose=True), "SY", 16)
        for h in range(NH):
            for dc in range(8):
                inc(sy.dma_start(out=vwT[:, dc * 1024 + h * 64: dc * 1024 + (h + 1) * 64],
                                 in_=wqkv_b.ap()[192 * h + 128:192 * h + 192,
                                                 dc * 128:(dc + 1) * 128],
                                 transpose=True), "SY", 16)

    for ck in range(8):
        for h in range(NH):
            bank = 5 + (h % 2)

            @block.tensor
            def _(t):
                if h == 0:
                    t.wait_ge(sem["SY"], C.get("SY"))
                t.wait_ge(sem["VE"], C.get("VE"))
                for dc in range(8):
                    mm = t.matmul(PS[bank][:, :],
                                  qkwT[:, (h * 8 + dc) * 128:(h * 8 + dc + 1) * 128],
                                  xn1t[:, dc * TOK1 + ck * 512: dc * TOK1 + (ck + 1) * 512],
                                  start=(dc == 0), stop=(dc == 7))
                inc(mm, "MM")

            @block.vector
            def _(v):
                v.wait_ge(sem["MM"], C.get("MM"))
                v.tensor_scalar(qT[:, h * 512:(h + 1) * 512], PS[bank][0:64, :],
                                qkb2[0:64, h:h + 1], 0.125, op0=ALU.add, op1=ALU.mult)
                inc(v.tensor_scalar(kT[:, h * 512:(h + 1) * 512], PS[bank][64:128, :],
                                    qkb2[64:128, h:h + 1], None, op0=ALU.add), "VE")

        for ts_ in range(4):
            @block.tensor
            def _(t):
                t.wait_ge(sem["SE"], C.get("SE"))
                t.wait_ge(sem["VE"], C.get("VE"))
                for hg in range(2):
                    for dc in range(8):
                        mm = t.matmul(PS[5 + hg][:, :],
                                      xn1t[:, dc * TOK1 + ck * 512 + ts_ * 128:
                                           dc * TOK1 + ck * 512 + (ts_ + 1) * 128],
                                      vwT[:, dc * 1024 + hg * 512: dc * 1024 + (hg + 1) * 512],
                                      start=(dc == 0), stop=(dc == 7))
                    inc(mm, "MM")

            @block.vector
            def _(v):
                for hg in range(2):
                    v.wait_ge(sem["MM"], C.get("MM") - (1 - hg))
                    inc(v.scalar_tensor_tensor(
                        vtok[:, ts_ * 1024 + hg * 512: ts_ * 1024 + (hg + 1) * 512],
                        PS[5 + hg][:, :], 1.0, vbias[:, hg * 512:(hg + 1) * 512],
                        op0=ALU.mult, op1=ALU.add), "VE")

        for hg in range(2):
            for hh in range(8):
                h = hg * 8 + hh

                @block.tensor
                def _(t):
                    t.wait_ge(sem["SE"], C.get("SE"))
                    t.wait_ge(sem["VE"], C.get("VE"))
                    for spb in range(4):
                        mm = t.matmul(PS[0][:, spb * 128:(spb + 1) * 128],
                                      kT[:, h * 512 + spb * 128: h * 512 + (spb + 1) * 128],
                                      qT[:, h * 512 + spb * 128: h * 512 + (spb + 1) * 128],
                                      start=True, stop=True)
                    inc(mm, "MM")

                @block.scalar
                def _(s):
                    s.wait_ge(sem["MM"], C.get("MM"))
                    s.wait_ge(sem["VE"], C.get("VE"))
                    inc(s.copy(xb6[:, :], PS[0][:, :]), "SE")

                @block.vector
                def _(v):
                    v.wait_ge(sem["SE"], C.get("SE"))
                    v.wait_ge(sem["MM"], C.get("MM"))
                    v.tensor_tensor(x26[:, :], xb6[:, :], xb6[:, :], op=ALU.mult)
                    v.tensor_scalar(i16[:, :], x26[:, :], 1.0 / 480.0, -1.0 / 48.0,
                                    op0=ALU.mult, op1=ALU.add)
                    v.tensor_tensor(i26[:, :], i16[:, :], x26[:, :], op=ALU.mult)
                    v.scalar_tensor_tensor(i16[:, :], i26[:, :], 0.25, xb6[:, :],
                                           op0=ALU.add, op1=ALU.mult)
                    v.tensor_scalar(i26[:, :], i16[:, :], gammac[:, h:h + 1],
                                    c2c[:, h:h + 1], op0=ALU.mult, op1=ALU.add)
                    inc(v.tensor_tensor(probsb[:, :], i26[:, :], bd16_s[:, :],
                                        op=ALU.mult), "VE")

                @block.tensor
                def _(t):
                    t.wait_ge(sem["VE"], C.get("VE"))
                    for spb in range(4):
                        mm = t.matmul(PS[1 + spb][:, hh * 64:(hh + 1) * 64],
                                      probsb[:, spb * 128:(spb + 1) * 128],
                                      vtok[:, spb * 1024 + h * 64: spb * 1024 + (h + 1) * 64],
                                      start=True, stop=True)
                    inc(mm, "MM")

            @block.vector
            def _(v):
                v.wait_ge(sem["MM"], C.get("MM"))
                for spb in range(4):
                    last = v.tensor_copy(
                        a1rows[:, spb * 1024 + hg * 512: spb * 1024 + (hg + 1) * 512],
                        PS[1 + spb][:, :])
                inc(last, "VE")

        @block.gpsimd
        def _(g):
            g.wait_ge(sem["VE"], C.get("VE"))
            for ts_ in range(4):
                r0 = ck * 512 + ts_ * 128
                inc(g.dma_start(out=part1.ap()[r0:r0 + 128, :],
                                in_=a1rows[:, ts_ * 1024:(ts_ + 1) * 1024]), "GP", 16)

    hard_barrier()
    p6.close()

    _scope_leave(nc, "l1_attn")
    _scope_enter(nc, "l1_shared")
    p6b = ExitStack()
    sgT = p6b.enter_context(nc.sbuf_tensor("sgT", [128, 8 * 1024], BF16))
    suT = p6b.enter_context(nc.sbuf_tensor("suT", [128, 8 * 1024], BF16))
    sdT = p6b.enter_context(nc.sbuf_tensor("sdT", [128, 8 * 1024], BF16))
    hh6 = p6b.enter_context(nc.sbuf_tensor("hh6", [128, 8 * 512], BF16))
    sil6 = p6b.enter_context(nc.sbuf_tensor("sil6", [128, 512], BF16))
    prow6 = p6b.enter_context(nc.sbuf_tensor("prow6", [128, 1024], BF16))
    a1ld = p6b.enter_context(nc.sbuf_tensor("a1ld", [128, 4 * 1024], BF16))

    @block.sync
    def _(sy):
        sy.wait_ge(sem["MM"], C.get("MM"))
        sy.wait_ge(sem["GP"], C.get("GP"))
        for (dstw, srcw) in ((sgT, sg_b), (suT, su_b), (sdT, sd_b)):
            for dc in range(8):
                inc(sy.dma_start(out=dstw[:, dc * 1024:(dc + 1) * 1024],
                                 in_=srcw.ap()[:, dc * 128:(dc + 1) * 128],
                                 transpose=True), "SY", 16)

    for ck in range(8):
        @block.sync
        def _(sy):
            sy.wait_ge(sem["VE"], C.get("VE"))
            for ts_ in range(4):
                r0 = ck * 512 + ts_ * 128
                inc(sy.dma_start(out=a1ld[:, ts_ * 1024:(ts_ + 1) * 1024],
                                 in_=part1.ap()[r0:r0 + 128, :]), "SY", 16)

        for oc in range(8):
            @block.tensor
            def _(t):
                t.wait_ge(sem["SE"], C.get("SE"))
                t.wait_ge(sem["VE"], C.get("VE"))
                if oc == 0:
                    t.wait_ge(sem["SY"], C.get("SY"))
                for dc in range(8):
                    mg = t.matmul(PS[5][:, :],
                                  sgT[:, dc * 1024 + oc * 128: dc * 1024 + oc * 128 + 128],
                                  xn1t[:, dc * TOK1 + ck * 512: dc * TOK1 + (ck + 1) * 512],
                                  start=(dc == 0), stop=(dc == 7))
                inc(mg, "MM")
                for dc in range(8):
                    mu = t.matmul(PS[6][:, :],
                                  suT[:, dc * 1024 + oc * 128: dc * 1024 + oc * 128 + 128],
                                  xn1t[:, dc * TOK1 + ck * 512: dc * TOK1 + (ck + 1) * 512],
                                  start=(dc == 0), stop=(dc == 7))
                inc(mu, "MM")

            @block.scalar
            def _(s):
                s.wait_ge(sem["MM"], C.get("MM") - 1)
                s.wait_ge(sem["VE"], C.get("VE"))
                inc(s.activation(sil6[:, :], PS[5][:, :], AF.Silu), "SE")

            @block.vector
            def _(v):
                v.wait_ge(sem["MM"], C.get("MM"))
                v.wait_ge(sem["SE"], C.get("SE"))
                inc(v.scalar_tensor_tensor(hh6[:, oc * 512:(oc + 1) * 512],
                                           PS[6][:, :], 1.0, sil6[:, :],
                                           op0=ALU.mult, op1=ALU.mult), "VE")

        for ts_ in range(4):
            @block.tensor
            def _(t):
                t.wait_ge(sem["VE"], C.get("VE"))
                t.wait_ge(sem["GP"], C.get("GP"))
                for dh in range(2):
                    for oc in range(8):
                        mm = t.matmul(PS[5 + dh][:, :],
                                      hh6[:, oc * 512 + ts_ * 128: oc * 512 + ts_ * 128 + 128],
                                      sdT[:, oc * 1024 + dh * 512: oc * 1024 + (dh + 1) * 512],
                                      start=(oc == 0), stop=(oc == 7))
                    inc(mm, "MM")

            @block.vector
            def _(v):
                v.wait_ge(sem["MM"], C.get("MM"))
                for dh in range(2):
                    last = v.scalar_tensor_tensor(
                        prow6[:, dh * 512:(dh + 1) * 512], PS[5 + dh][:, :], 1.0,
                        a1ld[:, ts_ * 1024 + dh * 512: ts_ * 1024 + (dh + 1) * 512],
                        op0=ALU.mult, op1=ALU.add)
                inc(last, "VE")

            @block.gpsimd
            def _(g):
                g.wait_ge(sem["VE"], C.get("VE"))
                r0 = ck * 512 + ts_ * 128
                inc(g.dma_start(out=part1.ap()[r0:r0 + 128, :], in_=prow6[:, :]), "GP", 16)

    hard_barrier()
    p6b.close()

    _scope_leave(nc, "l1_shared")
    _scope_enter(nc, "l1_moe")
    p7 = ExitStack()
    w1T = p7.enter_context(nc.sbuf_tensor("w1T", [128, 2 * 8 * 512], BF16))
    w3T = p7.enter_context(nc.sbuf_tensor("w3T", [128, 2 * 8 * 512], BF16))
    w2T = p7.enter_context(nc.sbuf_tensor("w2T", [128, 2 * 4 * 1024], BF16))
    xe1 = p7.enter_context(nc.sbuf_tensor("xe1", [128, 2 * (D + E)], BF16))
    xe2 = p7.enter_context(nc.sbuf_tensor("xe2", [128, 2 * (D + E)], BF16))
    xeT = p7.enter_context(nc.sbuf_tensor("xeT", [128, 8 * PAD], BF16))
    hh7 = p7.enter_context(nc.sbuf_tensor("hh7", [128, 4 * PAD], BF16))
    sil7 = p7.enter_context(nc.sbuf_tensor("sil7", [128, PAD], BF16))
    ye7 = p7.enter_context(nc.sbuf_tensor("ye7", [128, 1024], BF16))
    wc1 = p7.enter_context(nc.sbuf_tensor("wc1", [128, 4], F32))

    bcreg = [None]

    @block.gpsimd
    def _(g):
        bcreg[0] = g.alloc_register("bcreg")
        g.reg_mov(bcreg[0], TOK1)

    marks_w = {}
    marks_g = {}
    marks_st = {}
    mark_tr = {}
    mark_fc0 = {}
    DE = D + E

    def emit_wload(e):
        par = e % 2

        @block.sync
        def _(sy):
            if e == 0:
                sy.wait_ge(sem["CC"], C.get("CC"))
            else:
                sy.wait_ge(sem["MM"], mark_fc0[e - 1])
            for dc in range(8):
                inc(sy.dma_start(out=w1T[:, par * 4096 + dc * 512: par * 4096 + (dc + 1) * 512],
                                 in_=moe1.ap()[e][:, dc * 128:(dc + 1) * 128],
                                 transpose=True), "SY", 16)

        @block.scalar
        def _(s):
            if e == 0:
                s.wait_ge(sem["CC"], C.get("CC"))
            else:
                s.wait_ge(sem["MM"], mark_fc0[e - 1])
            for dc in range(8):
                inc(s.dma_start(out=w3T[:, par * 4096 + dc * 512: par * 4096 + (dc + 1) * 512],
                                in_=moe3.ap()[e][:, dc * 128:(dc + 1) * 128],
                                transpose=True), "SY", 16)
            for fc in range(4):
                inc(s.dma_start(out=w2T[:, par * 4096 + fc * 1024: par * 4096 + (fc + 1) * 1024],
                                in_=moe2.ap()[e][:, fc * 128:(fc + 1) * 128],
                                transpose=True), "SY", 16)
            marks_w[e] = C.get("SY")

    def emit_gather(e):
        par = e % 2

        @block.gpsimd
        def _(g):
            if e >= 1:
                g.wait_ge(sem["MM"], mark_tr[e - 1])
            inc(g.indirect_dma_start(
                xe1[:, par * DE:(par + 1) * DE], None, xn1tab_h.ap(),
                bass.IndirectOffsetOnAxis(ap=gidxT0[:, e:e + 1], axis=0),
                bounds_check=bcreg[0], oob_is_err=False), "GP", 16)
            inc(g.indirect_dma_start(
                xe2[:, par * DE:(par + 1) * DE], None, xn1tab_h.ap(),
                bass.IndirectOffsetOnAxis(ap=gidxT1[:, e:e + 1], axis=0),
                bounds_check=bcreg[0], oob_is_err=False), "GP", 16)
            marks_g[e] = C.get("GP")

    emit_wload(0)
    emit_gather(0)
    for e in range(E):
        par = e % 2

        @block.tensor
        def _(t):
            t.wait_ge(sem["GP"], marks_g[e])
            t.wait_ge(sem["VE"], C.get("VE"))
            for dc in range(8):
                mm = t.transpose(PS[7][:, :].bitcast(BF16)[:, dc * 128:(dc + 1) * 128],
                                 xe1[:, par * DE + dc * 128: par * DE + (dc + 1) * 128], ID)
            mark_tr[e] = inc(mm, "MM")
            for dc in range(8):
                mm = t.transpose(PS[0][:, :].bitcast(BF16)[:, dc * 64:(dc + 1) * 64],
                                 xe2[0:64, par * DE + dc * 128: par * DE + (dc + 1) * 128],
                                 ID[0:64, 0:64])
            inc(mm, "MM")

        if e + 1 < E:
            emit_gather(e + 1)

        @block.vector
        def _(v):
            v.wait_ge(sem["MM"], C.get("MM"))
            inc(v.tensor_copy(
                xeT[:, :].rearrange("p (a s) -> p a s", a=8)[:, :, 0:128],
                PS[7][:, :].bitcast(BF16).rearrange("p (a s) -> p a s", a=8)), "VE")
            inc(v.tensor_copy(
                xeT[:, :].rearrange("p (a s) -> p a s", a=8)[:, :, 128:192],
                PS[0][:, :].bitcast(BF16)[:, 0:512].rearrange("p (a s) -> p a s", a=8)),
                "VE")
            inc(v.tensor_copy(wc1[:, 0:1], xe1[:, par * DE + D + e: par * DE + D + e + 1]),
                "VE")
            inc(v.tensor_copy(wc1[0:64, 1:2], xe2[0:64, par * DE + D + e: par * DE + D + e + 1]),
                "VE")
            vw(v)

        for fc in range(4):
            @block.tensor
            def _(t):
                t.wait_ge(sem["VE"], C.get("VE"))
                t.wait_ge(sem["SE"], C.get("SE"))
                if fc == 0:
                    t.wait_ge(sem["SY"], marks_w[e])
                for dc in range(8):
                    m1 = t.matmul(PS[1][:, 0:PAD],
                                  w1T[:, par * 4096 + dc * 512 + fc * 128: par * 4096 + dc * 512 + fc * 128 + 128],
                                  xeT[:, dc * PAD:(dc + 1) * PAD],
                                  start=(dc == 0), stop=(dc == 7))
                mk = inc(m1, "MM")
                if fc == 0:
                    mark_fc0[e] = mk
                for dc in range(8):
                    m3 = t.matmul(PS[2][:, 0:PAD],
                                  w3T[:, par * 4096 + dc * 512 + fc * 128: par * 4096 + dc * 512 + fc * 128 + 128],
                                  xeT[:, dc * PAD:(dc + 1) * PAD],
                                  start=(dc == 0), stop=(dc == 7))
                inc(m3, "MM")

            if fc == 0 and e + 1 < E:
                emit_wload(e + 1)

            @block.scalar
            def _(s):
                s.wait_ge(sem["MM"], C.get("MM") - 1)
                s.wait_ge(sem["VE"], C.get("VE"))
                inc(s.activation(sil7[:, :], PS[1][:, 0:PAD], AF.Silu), "SE")

            @block.vector
            def _(v):
                v.wait_ge(sem["MM"], C.get("MM"))
                v.wait_ge(sem["SE"], C.get("SE"))
                inc(v.scalar_tensor_tensor(hh7[:, fc * PAD:(fc + 1) * PAD],
                                           PS[2][:, 0:PAD], 1.0, sil7[:, :],
                                           op0=ALU.mult, op1=ALU.mult), "VE")

        for st_ in range(2):
            rows = 128 if st_ == 0 else 64

            @block.tensor
            def _(t):
                t.wait_ge(sem["VE"], C.get("VE"))
                for dh in range(2):
                    for fc in range(4):
                        mm = t.matmul(PS[3 + dh][0:rows, :],
                                      hh7[:, fc * PAD + st_ * 128: fc * PAD + st_ * 128 + rows],
                                      w2T[:, par * 4096 + fc * 1024 + dh * 512: par * 4096 + fc * 1024 + (dh + 1) * 512],
                                      start=(fc == 0), stop=(fc == 3))
                    inc(mm, "MM")

            @block.vector
            def _(v):
                v.wait_ge(sem["MM"], C.get("MM"))
                if (e, st_) > (0, 0):
                    prev = (e, st_ - 1) if st_ == 1 else (e - 1, 1)
                    v.wait_ge(sem["ST"], marks_st[prev])
                for dh in range(2):
                    last = v.tensor_scalar(ye7[0:rows, dh * 512:(dh + 1) * 512],
                                           PS[3 + dh][0:rows, :], wc1[0:rows, st_:st_ + 1],
                                           None, op0=ALU.mult)
                inc(last, "VE")

            @block.gpsimd
            def _(g):
                g.wait_ge(sem["VE"], C.get("VE"))
                r0 = 1 + e * PAD + st_ * 128
                marks_st[(e, st_)] = inc(
                    g.dma_start(out=yetab.ap()[r0:r0 + rows, :], in_=ye7[0:rows, :]),
                    "ST", 16)

    hard_barrier()
    p7.close()

    _scope_leave(nc, "l1_moe")
    _scope_enter(nc, "l1_final")
    p8 = ExitStack()
    acc8 = p8.enter_context(nc.sbuf_tensor("acc8", [128, 2 * 8 * 1024], BF16))
    x1r8 = p8.enter_context(nc.sbuf_tensor("x1r8", [128, 2 * 1024], F32))
    p1r8 = p8.enter_context(nc.sbuf_tensor("p1r8", [128, 2 * 1024], BF16))
    sum8 = p8.enter_context(nc.sbuf_tensor("sum8", [128, 4 * 1024], BF16))
    yt8 = p8.enter_context(nc.sbuf_tensor("yt8", [128, 2 * 1024], F32))

    @block.gpsimd
    def _(g):
        g.reg_mov(bcreg[0], E * PAD)

    marks8v = {}
    mark8w = {}

    def emit_p8_loads(tt):
        par = tt % 2

        @block.gpsimd
        def _(g):
            if tt >= 1:
                g.wait_ge(sem["VE"], mark8w[tt - 1])
            for r in range(8):
                inc(g.indirect_dma_start(
                    acc8[:, (par * 8 + r) * 1024:(par * 8 + r + 1) * 1024], None,
                    yetab.ap(),
                    bass.IndirectOffsetOnAxis(ap=destbuf[:, tt * 8 + r: tt * 8 + r + 1],
                                              axis=0),
                    bounds_check=bcreg[0], oob_is_err=False), "GP", 16)

        @block.sync
        def _(sy):
            if tt >= 1:
                sy.wait_ge(sem["VE"], mark8w[tt - 1])
            for ss in range(8):
                inc(sy.dma_start(out=x1r8[ss * 16:(ss + 1) * 16, par * 1024:(par + 1) * 1024],
                                 in_=x1v[:, 8 * tt + ss, :]), "SY", 16)
            inc(sy.dma_start(out=p1r8[:, par * 1024:(par + 1) * 1024],
                             in_=part1.ap()[tt * 128:(tt + 1) * 128, :]), "SY", 16)

    emit_p8_loads(0)
    for tt in range(32):
        par = tt % 2

        @block.vector
        def _(v):
            v.wait_ge(sem["GP"], C.get("GP"))
            v.wait_ge(sem["SY"], C.get("SY"))
            ab = par * 8 * 1024
            yb = par * 1024
            i_ = 0
            mark8w[tt] = inc(v.tensor_tensor(
                sum8[:, i_ * 1024:(i_ + 1) * 1024],
                acc8[:, ab + 2 * i_ * 1024: ab + (2 * i_ + 1) * 1024],
                acc8[:, ab + (2 * i_ + 1) * 1024: ab + (2 * i_ + 2) * 1024],
                op=ALU.add), "VE")
            for i_ in range(1, 4):
                v.tensor_tensor(sum8[:, i_ * 1024:(i_ + 1) * 1024],
                                acc8[:, ab + 2 * i_ * 1024: ab + (2 * i_ + 1) * 1024],
                                acc8[:, ab + (2 * i_ + 1) * 1024: ab + (2 * i_ + 2) * 1024],
                                op=ALU.add)
            v.tensor_tensor(sum8[:, 0:1024], sum8[:, 0:1024], sum8[:, 1024:2048], op=ALU.add)
            v.tensor_tensor(sum8[:, 2048:3072], sum8[:, 2048:3072], sum8[:, 3072:4096], op=ALU.add)
            v.tensor_tensor(yt8[:, yb:yb + 1024], x1r8[:, par * 1024:(par + 1) * 1024],
                            p1r8[:, par * 1024:(par + 1) * 1024], op=ALU.add)
            v.tensor_tensor(yt8[:, yb:yb + 1024], yt8[:, yb:yb + 1024], sum8[:, 0:1024],
                            op=ALU.add)
            marks8v[tt] = inc(v.tensor_tensor(yt8[:, yb:yb + 1024], yt8[:, yb:yb + 1024],
                                              sum8[:, 2048:3072], op=ALU.add), "VE")

        if tt + 1 < 32:
            emit_p8_loads(tt + 1)

        @block.gpsimd
        def _(g):
            g.wait_ge(sem["VE"], C.get("VE"))
            for ss in range(8):
                inc(g.dma_start(
                    out=y_out[:, 8 * tt + ss, :],
                    in_=yt8[ss * 16 + 0:(ss + 1) * 16, par * 1024: par * 1024 + 1024]),
                    "GP", 16)

    _scope_leave(nc, "l1_final")
    hard_barrier()
    if "dbg_xn1" in env:
        @block.sync
        def _(sy):
            sy.wait_ge(sem["GP"], C.get("GP"))
            inc(sy.dma_start(out=env["dbg_xn1"], in_=xn1tab_h.ap()[:, :]), "SY", 16)
            inc(sy.dma_start(out=env["dbg_part1"], in_=part1.ap()[:, :]), "SY", 16)
            inc(sy.dma_start(out=env["dbg_ye"], in_=yetab.ap()[:, :]), "SY", 16)
            inc(sy.dma_start(out=env["dbg_gidx"][:, 0:64], in_=gidxT0[:, :]), "SY", 16)
            inc(sy.dma_start(out=env["dbg_gidx"][:, 64:128], in_=gidxT1[:, :]), "SY", 16)
            inc(sy.dma_start(out=env["dbg_dest"], in_=destbuf[:, :]), "SY", 16)
            sy.wait_ge(sem["SY"], C.get("SY"))
        hard_barrier()
    p8.close()
    l1c.close()

    @block.gpsimd
    def _(g):
        g.wait_ge(sem["GP"], C.get("GP"))
        g.wait_ge(sem["SY"], C.get("SY"))
        g.wait_ge(sem["VE"], C.get("VE"))
        g.wait_ge(sem["MM"], C.get("MM"))
        g.wait_ge(sem["SE"], C.get("SE"))
        g.wait_ge(sem["ST"], C.get("ST"))



def make_consts(core_id):
    ID = np.eye(128)
    triu = np.triu(np.full((128, 128), NEG, np.float32), 1)
    tril = np.tril(np.full((128, 128), NEG, np.float32), -1)
    sv = np.arange(128) // 16
    bv = np.arange(128) % 16
    u_tile = ((bv[:, None] == bv[None, :]) & (sv[:, None] < sv[None, :])).astype(np.float32)
    u_plain = (np.arange(128)[:, None] < np.arange(128)[None, :]).astype(np.float32)
    eb = (bv[:, None] == np.arange(16)[None, :]).astype(np.float32)
    ebt = np.ascontiguousarray(eb.T)
    jv = np.arange(128) // 16
    bpv = np.arange(128) % 16
    u1 = ((bpv[:, None] < np.arange(16)[None, :]) |
          ((bpv[:, None] == np.arange(16)[None, :]) & (jv[:, None] < core_id))).astype(np.float32)
    bd = (np.arange(128)[:, None] // 16 == (np.arange(512)[None, :] % 128) // 16)
    return {
        "id_bf": ID.astype(ml_dtypes.bfloat16),
        "id_f32": ID.astype(np.float32),
        "triu_bf": triu.astype(ml_dtypes.bfloat16),
        "tril_bf": tril.astype(ml_dtypes.bfloat16),
        "u_tile": u_tile,
        "u_plain": u_plain,
        "eb_mat": eb,
        "ebt_mat": ebt,
        "u1_mine": u1,
        "e192_row": (np.arange(E) * PAD + 1).astype(np.float32)[None, :],
        "bd16_mask": bd.astype(ml_dtypes.bfloat16),
        "iota_col": np.arange(128, dtype=np.float32)[:, None],
    }


def make_in_maps(inputs, l0_only=False, l1_only=False, x1=None):
    maps = []
    for c in range(NCORES):
        cst = make_consts(c)
        m = {}
        for k in ["id_bf", "id_f32", "triu_bf", "tril_bf"]:
            m[k] = cst[k]
        m["x_slice"] = np.ascontiguousarray(inputs["L0_attn_x"][2 * c:2 * c + 2])
        for k in ["L0_attn_P_q", "L0_attn_P_k", "L0_attn_P_v", "L0_attn_P_o",
                  "L0_ffn_w_gate", "L0_ffn_w_up", "L0_ffn_w_down"]:
            m[k] = np.ascontiguousarray(inputs[k])
        if not l0_only:
            for k in ["u_tile", "u_plain", "eb_mat", "ebt_mat", "u1_mine",
                      "e192_row", "bd16_mask", "iota_col"]:
                m[k] = cst[k]
            m["L1_attn_Wqkv"] = np.ascontiguousarray(inputs["L1_attn_Wqkv"])
            m["L1_attn_bqkv"] = np.ascontiguousarray(inputs["L1_attn_bqkv"]).reshape(1, -1)
            m["L1_attn_seq_gamma"] = np.ascontiguousarray(inputs["L1_attn_seq_gamma"]).reshape(1, -1)
            m["L1_attn_seq_beta"] = np.ascontiguousarray(inputs["L1_attn_seq_beta"]).reshape(1, -1)
            m["L1_ffn_gate_weight"] = np.ascontiguousarray(inputs["L1_ffn_gate_weight"])
            m["L1_ffn_e_bias"] = np.ascontiguousarray(inputs["L1_ffn_e_bias"]).reshape(1, -1)
            ne = E // NCORES
            m["w1_my"] = np.ascontiguousarray(inputs["L1_ffn_w1"][c * ne:(c + 1) * ne])
            m["w2_my"] = np.ascontiguousarray(inputs["L1_ffn_w2"][c * ne:(c + 1) * ne])
            m["w3_my"] = np.ascontiguousarray(inputs["L1_ffn_w3"][c * ne:(c + 1) * ne])
            m["L1_ffn_shared_gate"] = np.ascontiguousarray(inputs["L1_ffn_shared_gate"])
            m["L1_ffn_shared_up"] = np.ascontiguousarray(inputs["L1_ffn_shared_up"])
            m["L1_ffn_shared_down"] = np.ascontiguousarray(inputs["L1_ffn_shared_down"])
            if l1_only:
                m["x1_slice"] = np.ascontiguousarray(
                    x1[:, c * SPC:(c + 1) * SPC, :].reshape(TOK1, D))
        maps.append(m)
    return maps


def kernel(**inputs):
    inputs = {k: np.asarray(v) for k, v in inputs.items()}
    nc = build_program()
    in_maps = make_in_maps(inputs)
    res = run_bass_kernel_spmd(nc, in_maps, core_ids=list(range(NCORES)))
    out = np.empty((B, S, D), np.float32)
    for c in range(NCORES):
        out[:, c * SPC:(c + 1) * SPC, :] = res.results[c]["y_slice"]
    return out
